# version 26
# baseline (speedup 1.0000x reference)
import numpy as np
import ml_dtypes
from contextlib import ExitStack
from collections import defaultdict

import concourse.bass as bass
import concourse.mybir as mybir
from concourse.bass_utils import run_bass_kernel_spmd

F32 = mybir.dt.float32
BF16 = mybir.dt.bfloat16
AF = mybir.ActivationFunctionType
ALU = mybir.AluOpType
AX = mybir.AxisListType
BF16NP = ml_dtypes.bfloat16

T = 1024
NIF = 16
H = 16
NBLK = 4
DFF = 5504
NHF = 43
NCORES = 8
MSK_NEG = -1.0e4
WSW = 4352

TAPS = {
    "xl": ((128, 16384), BF16), "q": ((128, 16384), BF16), "k": ((128, 16384), BF16),
    "vt": ((128, 16384), BF16), "u": ((128, 16384), BF16), "kvio": ((128, 2048), F32),
    "sin": ((128, 2048), F32), "o": ((128, 16384), BF16), "go": ((128, 16384), BF16),
    "aout": ((128, 16384), F32), "xmid": ((2048, 1024), BF16), "y": ((128, 16384), BF16),
    "ewx": ((64, 1024), F32), "lg": ((72, 1024), BF16), "lu": ((72, 1024), BF16),
    "hida": ((128, 16384), BF16), "hidb": ((128, 16384), BF16), "hidc": ((128, 11264), BF16),
    "ld": ((72, 1024), BF16), "f": ((2048, 1024), BF16),
    "sct": ((128, 64), F32), "bit": ((128, 64), F32), "ewt": ((128, 64), F32),
    "ewf": ((8, 1024), BF16),
}


class Em:
    def __init__(self, nc, ctx):
        self.nc = nc
        self.cnt = defaultdict(int)
        self.water = defaultdict(int)
        self.sems = {}
        names = (["T", "V", "S", "C"] + [f"gw{i}" for i in range(8)]
                 + [f"dx{i}" for i in range(8)] + [f"ds{i}" for i in range(8)])
        for n in names:
            self.sems[n] = ctx.enter_context(nc.semaphore(n))

    def wait(self, eng, engname, sem, val):
        if val <= 0:
            return
        if self.water[(engname, sem)] >= val:
            return
        eng.wait_ge(self.sems[sem], val)
        self.water[(engname, sem)] = val

    def inc(self, inst, sem, n=1):
        self.cnt[sem] += n
        inst.then_inc(self.sems[sem], n)
        return self.cnt[sem]

    def barrier(self, eng, engname):
        for s in ["T", "V", "S"] + [f"dx{i}" for i in range(8)]:
            self.wait(eng, engname, s, self.cnt[s])


class Ring:
    def __init__(self, em, prefix, n=8):
        self.em = em
        self.names = [f"{prefix}{i}" for i in range(n)]
        self.k = 0
        self.n = n

    def pre(self, eng, engname):
        nm = self.names[self.k % self.n]
        v = self.em.cnt[nm]
        if v > 0:
            self.em.wait(eng, engname, nm, v)

    def issue(self, inst):
        nm = self.names[self.k % self.n]
        self.k += 1
        v = self.em.inc(inst, nm, 16)
        return (nm, v)

    def last(self):
        nm = self.names[(self.k - 1) % self.n]
        return (nm, self.em.cnt[nm])


def _host_consts(half):
    c = {}
    c["c_ident_bf"] = np.eye(128, dtype=np.float32)
    c["c_ident_f"] = np.eye(128, dtype=np.float32)
    c["c_ones_bf"] = np.ones((128, 1), dtype=np.float32)
    c["c_ones_f"] = np.ones((1, 128), dtype=np.float32)
    e8 = np.zeros((8, 64), dtype=np.float32)
    for e in range(8):
        e8[e, e * 8:(e + 1) * 8] = 1
    c["c_e8"] = e8
    g = np.zeros((128, 2, 256), dtype=np.float32)
    for sh in range(2):
        sglob = sh * 128 + np.arange(128)[:, None]
        m = np.arange(256)[None, :]
        g[:, sh, :] = np.where(m >= sglob, -(m - sglob).astype(np.float32), MSK_NEG)
    c["c_gneg"] = np.ascontiguousarray(g.reshape(128, 512))
    c["c_onesr"] = np.ones((1, 128), dtype=np.float32)
    kd = np.zeros((128, 2), dtype=np.float32)
    for sh in range(2):
        kd[:, sh] = (sh * 128 + np.arange(128) - 255).astype(np.float32)
    c["c_kdecl"] = kd
    kg = np.zeros((128, 8), dtype=np.float32)
    for ch in range(8):
        kg[:, ch] = (ch * 128 + np.arange(128) - 1023).astype(np.float32)
    c["c_kdecg"] = kg
    c["c_mask"] = np.full((128, 1), 1.0 if half == 1 else 0.0, dtype=np.float32)
    return c


def build(taps=()):
    nc = bass.Bass("TRN2", target_bir_lowering=False, detect_race_conditions=False)
    DT = {}

    def din(name, shape, dtype=F32):
        DT[name] = nc.dram_tensor(name, list(shape), dtype, kind="ExternalInput").ap()
        return DT[name]

    x_fm = din("x_fm", (2048, 1024))
    wfm = din("wfm", (2048, 6144))
    wv = din("wv", (2048, 2048))
    wout = din("wout", (2048, 2048))
    wg = din("wg", (2048, 5504))
    wu2 = din("wu2", (2048, 5504))
    wd = din("wd", (5504, 2048))
    wr = din("wr", (2048, 8))
    acg = din("acg", (2048, 72))
    acu = din("acu", (2048, 72))
    acd = din("acd", (5504, 72))
    bcg = din("bcg", (72, 5504))
    bcu = din("bcu", (72, 5504))
    bcd = din("bcd", (72, 2048))
    slope = din("slope", (1, 16))
    ebias = din("ebias", (8, 1))
    c_in = {}
    for nm, shp, dt in [("c_ident_bf", (128, 128), F32), ("c_ident_f", (128, 128), F32),
                        ("c_ones_bf", (128, 1), F32), ("c_ones_f", (1, 128), F32),
                        ("c_e8", (8, 64), F32), ("c_gneg", (128, 512), F32),
                        ("c_onesr", (1, 128), F32), ("c_kdecl", (128, 2), F32),
                        ("c_kdecg", (128, 8), F32), ("c_mask", (128, 1), F32)]:
        c_in[nm] = din(nm, shp, dt)

    out_fm = nc.dram_tensor("out_fm", [2048, 1024], F32, kind="ExternalOutput").ap()
    for tp in taps:
        shp, dt = TAPS[tp]
        DT["tap_" + tp] = nc.dram_tensor("tap_" + tp, list(shp), dt, kind="ExternalOutput").ap()

    xmid_sp = nc.dram_tensor("xmid_sp", [2048, 1024], BF16).ap()
    cc_in = nc.dram_tensor("cc_in", [128, 2048], F32)
    cc_out = nc.dram_tensor("cc_out", [256, 2048], F32)

    with ExitStack() as ctx:
        em = Em(nc, ctx)
        gw = Ring(em, "gw")
        dx = Ring(em, "dx")
        ds = Ring(em, "ds")

        def SBx(c, n, shp, dt=F32):
            return c.enter_context(nc.sbuf_tensor(n, list(shp), dt))

        def PSx(c, n, shp):
            return c.enter_context(nc.psum_tensor(n, list(shp), F32))

        ident_bf = SBx(ctx, "ident_bf", (128, 128), BF16)
        ident_f = SBx(ctx, "ident_f", (128, 128))
        ones_bf = SBx(ctx, "ones_bf", (128, 1), BF16)
        ones_fc = SBx(ctx, "ones_fc", (128, 1))
        ones_f = SBx(ctx, "ones_f", (1, 128))
        e8_sb = SBx(ctx, "e8_sb", (8, 64), BF16)
        gneg_sb = SBx(ctx, "gneg_sb", (128, 512), BF16)
        ones_bfr = SBx(ctx, "ones_bfr", (1, 128), BF16)
        e1b = SBx(ctx, "e1b", (128, 16))
        kdecl_sb = SBx(ctx, "kdecl_sb", (128, 2))
        kdecg_sb = SBx(ctx, "kdecg_sb", (128, 8))
        mask_sb = SBx(ctx, "mask_sb", (128, 1))
        slope_sb = SBx(ctx, "slope_sb", (1, 16))
        ebias_sb = SBx(ctx, "ebias_sb", (8, 1))
        sbcast = SBx(ctx, "sbcast", (128, 16))
        bdbcast = SBx(ctx, "bdbcast", (128, 16))
        WS = [SBx(ctx, f"ws{i}", (128, WSW), BF16) for i in range(3)]
        Ubuf = SBx(ctx, "ubuf", (128, 16384), BF16)
        XLbuf = SBx(ctx, "xlbuf", (128, 16384), BF16)

        block = ctx.enter_context(nc.Block())

        def _(g):
            for nm, dst in [("c_ident_bf", ident_bf), ("c_ident_f", ident_f),
                            ("c_ones_bf", ones_bf), ("c_ones_bf", ones_fc),
                            ("c_ones_f", ones_f),
                            ("c_e8", e8_sb), ("c_gneg", gneg_sb),
                            ("c_onesr", ones_bfr), ("c_kdecl", kdecl_sb),
                            ("c_kdecg", kdecg_sb), ("c_mask", mask_sb)]:
                (gw.pre(g, "g") or gw.issue(g.dma_start(out=dst[:, :], in_=c_in[nm])))
            (gw.pre(g, "g") or gw.issue(g.dma_start(out=slope_sb[:, :], in_=slope)))
            (gw.pre(g, "g") or gw.issue(g.dma_start(out=ebias_sb[:, :], in_=ebias)))
        block.gpsimd(_)
        consts_ready = [(nm, em.cnt[nm]) for nm in gw.names if em.cnt[nm] > 0]

        def wait_consts(eng, engname):
            for nm, v in consts_ready:
                em.wait(eng, engname, nm, v)

        with ExitStack() as ph0:
            p_sb = PSx(ph0, "p_sb", (128, 32))

            def _(t):
                wait_consts(t, "t")
                t.matmul(p_sb[:, 0:16], ones_f[0:1, :], slope_sb[0:1, :], start=True, stop=True)
                em.inc(t.matmul(p_sb[:, 16:32], ones_f[0:1, :], slope_sb[0:1, :], start=True, stop=True), "T")
            block.tensor(_)
            tv = em.cnt["T"]

            def _(v):
                em.wait(v, "v", "T", tv)
                v.tensor_copy(out=sbcast[:, :], in_=p_sb[:, 0:16])
                v.tensor_scalar(out=e1b[:, :], in0=p_sb[:, 0:16], scalar1=-1.0,
                                scalar2=None, op0=ALU.mult)
                em.inc(v.tensor_scalar(out=bdbcast[:, :], in0=p_sb[:, 16:32], scalar1=-256.0,
                                       scalar2=None, op0=ALU.mult), "V")
            block.vector(_)
            vv = em.cnt["V"]

            def _(s):
                em.wait(s, "s", "V", vv)
                s.activation(out=e1b[:, :], in_=e1b[:, :], func=AF.Exp)
                em.inc(s.activation(out=bdbcast[:, :], in_=bdbcast[:, :], func=AF.Exp), "S")
            block.scalar(_)
        BC_V = em.cnt["V"]
        BC_S = em.cnt["S"]

        def ln_finalize(p_s, p_q, rowm, rowr, tn1, eps, stat_T):
            def _(v):
                em.wait(v, "v", "T", stat_T)

                def step(inst):
                    em.inc(inst, "V")
                    em.wait(v, "v", "V", em.cnt["V"])
                step(v.tensor_scalar(out=rowm[0:1, :], in0=p_s[0:1, :], scalar1=1.0 / 2048, scalar2=None, op0=ALU.mult))
                step(v.tensor_scalar(out=rowr[0:1, :], in0=p_q[0:1, :], scalar1=1.0 / 2048, scalar2=None, op0=ALU.mult))
                step(v.tensor_mul(out=tn1[0:1, :], in0=rowm[0:1, :], in1=rowm[0:1, :]))
                step(v.tensor_sub(out=rowr[0:1, :], in0=rowr[0:1, :], in1=tn1[0:1, :]))
                em.inc(v.tensor_scalar(out=rowr[0:1, :], in0=rowr[0:1, :], scalar1=eps, scalar2=None, op0=ALU.add), "V")
            block.vector(_)
            v1 = em.cnt["V"]

            def _(s):
                em.wait(s, "s", "V", v1)
                em.inc(s.sqrt(out=rowr[0:1, :], in_=rowr[0:1, :]), "S")
            block.scalar(_)
            s1 = em.cnt["S"]

            def _(v):
                em.wait(v, "v", "S", s1)
                em.inc(v.reciprocal(out=rowr[0:1, :], in_=rowr[0:1, :]), "V")
                em.wait(v, "v", "V", em.cnt["V"])
                em.inc(v.tensor_mul(out=rowm[0:1, :], in0=rowm[0:1, :], in1=rowr[0:1, :]), "V")
            block.vector(_)
            return em.cnt["V"]

        def ln_bcast(p_R, p_MR, rowr, rowm, rows_V):
            def _(t):
                em.wait(t, "t", "V", rows_V)
                for half in range(2):
                    t.matmul(p_R[:, half * 512:(half + 1) * 512], ones_f[0:1, :],
                             rowr[0:1, half * 512:(half + 1) * 512], start=True, stop=True)
                    i = t.matmul(p_MR[:, half * 512:(half + 1) * 512], ones_f[0:1, :],
                                 rowm[0:1, half * 512:(half + 1) * 512], start=True, stop=True)
                em.inc(i, "T")
            block.tensor(_)
            return em.cnt["T"]

        with ExitStack() as ph1:
            xbuf = SBx(ph1, "xbuf", (128, 16384))
            xsq = SBx(ph1, "xsq", (128, 2048), BF16)
            tn = SBx(ph1, "tn", (128, 1024))
            rowm = SBx(ph1, "rowm", (1, 1024))
            rowr = SBx(ph1, "rowr", (1, 1024))
            p_s = PSx(ph1, "p_s1", (1, 1024))
            p_q = PSx(ph1, "p_q1", (1, 1024))
            p_R = PSx(ph1, "p_R1", (128, 1024))
            p_MR = PSx(ph1, "p_MR1", (128, 1024))

            xdma = [None] * NIF
            def _(sy):
                for c in range(0, NIF, 2):
                    xdma[c] = (dx.pre(sy, "sy") or dx.issue(sy.dma_start(
                        out=xbuf[:, c * 1024:(c + 1) * 1024],
                        in_=x_fm[c * 128:(c + 1) * 128, :])))
            block.sync(_)
            def _(sc):
                for c in range(1, NIF, 2):
                    xdma[c] = (ds.pre(sc, "s") or ds.issue(sc.dma_start(
                        out=xbuf[:, c * 1024:(c + 1) * 1024],
                        in_=x_fm[c * 128:(c + 1) * 128, :])))
            block.scalar(_)

            stat_T = []
            for c in range(NIF):
                par = c % 2
                def _(v, c=c, par=par):
                    nm, vv = xdma[c]
                    em.wait(v, "v", nm, vv)
                    if c >= 2:
                        em.wait(v, "v", "T", stat_T[c - 2])
                    em.inc(v.tensor_mul(out=xsq[:, par * 1024:(par + 1) * 1024],
                                        in0=xbuf[:, c * 1024:(c + 1) * 1024],
                                        in1=xbuf[:, c * 1024:(c + 1) * 1024]), "V")
                block.vector(_)
                sqv = em.cnt["V"]

                def _(t, c=c, par=par, sqv=sqv):
                    em.wait(t, "t", "V", sqv)
                    em.wait(t, "t", "V", BC_V)
                    for half in range(2):
                        t.matmul(p_s[0:1, half * 512:(half + 1) * 512], ones_fc[:, :],
                                 xbuf[:, c * 1024 + half * 512: c * 1024 + (half + 1) * 512],
                                 start=(c == 0), stop=(c == NIF - 1))
                    for half in range(2):
                        i = t.matmul(p_q[0:1, half * 512:(half + 1) * 512], ones_bf[:, :],
                                     xsq[:, par * 1024 + half * 512: par * 1024 + (half + 1) * 512],
                                     start=(c == 0), stop=(c == NIF - 1))
                    em.inc(i, "T")
                block.tensor(_)
                stat_T.append(em.cnt["T"])

            rows_V = ln_finalize(p_s, p_q, rowm, rowr, tn, 1e-5, stat_T[-1])
            bc_T = ln_bcast(p_R, p_MR, rowr, rowm, rows_V)

            xl_list = []
            def _(v):
                em.wait(v, "v", "T", bc_T)
                for c in range(NIF):
                    v.tensor_mul(out=tn[:, :], in0=xbuf[:, c * 1024:(c + 1) * 1024], in1=p_R[:, :])
                    em.inc(v.tensor_sub(out=XLbuf[:, c * 1024:(c + 1) * 1024], in0=tn[:, :], in1=p_MR[:, :]), "V")
                    xl_list.append(em.cnt["V"])
            block.vector(_)
            XL_READY = em.cnt["V"]

            if "xl" in taps:
                def _(sy):
                    em.wait(sy, "sy", "V", XL_READY)
                    (dx.pre(sy, "sy") or dx.issue(sy.dma_start(out=DT["tap_xl"], in_=XLbuf[:, :])))
                block.sync(_)

        def fm_mains(src_dram, n_oc, evac, rhs_buf, rhs_ready_V, psA, psB, ps_free0, ws_free0_T,
                     rhs_list=None):
            nslab = n_oc // 2
            slab_dma = []
            slab_cons = {}
            psum_free = {}
            for sl in range(nslab):
                def _(g, sl=sl):
                    if sl >= 3:
                        em.wait(g, "g", "T", slab_cons[sl - 3])
                    else:
                        em.wait(g, "g", "T", ws_free0_T)
                    src = src_dram[:, sl * 256:(sl + 1) * 256]
                    gw.pre(g, "g")
                    i = g.dma_start(
                        out=WS[sl % 3][:, 0:4096].rearrange("p (c o) -> p c o", c=16),
                        in_=src.rearrange("(c p) o -> p c o", p=128))
                    slab_dma.append(gw.issue(i))
                block.gpsimd(_)

                for oc2 in range(2):
                    oc = 2 * sl + oc2
                    for half in range(2):
                        ps = psA if oc % 2 == 0 else psB
                        def _(t, sl=sl, oc=oc, oc2=oc2, half=half, ps=ps):
                            nm, v = slab_dma[sl]
                            em.wait(t, "t", nm, v)
                            if rhs_list is None:
                                em.wait(t, "t", "V", rhs_ready_V)
                            key = (oc - 2, half)
                            if key in psum_free:
                                for sem, val in psum_free[key]:
                                    em.wait(t, "t", sem, val)
                            else:
                                for sem, val in ps_free0:
                                    em.wait(t, "t", sem, val)
                            for c in range(16):
                                if rhs_list is not None:
                                    em.wait(t, "t", "V", rhs_list[c])
                                i = t.matmul(ps[:, half * 512:(half + 1) * 512],
                                             WS[sl % 3][:, c * 256 + oc2 * 128: c * 256 + oc2 * 128 + 128],
                                             rhs_buf[:, c * 1024 + half * 512: c * 1024 + (half + 1) * 512],
                                             start=(c == 0), stop=(c == 15))
                            em.inc(i, "T")
                        block.tensor(_)
                        mm_T = em.cnt["T"]
                        slab_cons[sl] = mm_T
                        ps = psA if oc % 2 == 0 else psB
                        fr = evac(oc, half, ps[:, half * 512:(half + 1) * 512], mm_T)
                        psum_free[(oc, half)] = [fr] if isinstance(fr, tuple) else fr
            return em.cnt["T"]

        with ExitStack() as ph2:
            Qbuf = SBx(ph2, "qbuf", (128, 16384), BF16)
            Kbuf = SBx(ph2, "kbuf", (128, 16384), BF16)
            VTbuf = SBx(ph2, "vtbuf", (128, 16384), BF16)
            KVIO = SBx(ph2, "kvio", (128, 2048))
            tbl = SBx(ph2, "tbl", (128, 1024), BF16)
            kdl = SBx(ph2, "kdl", (128, 6))
            kdg = SBx(ph2, "kdg", (128, 16))
            kvst = SBx(ph2, "kvst", (128, 384))
            kvbf = SBx(ph2, "kvbf", (128, 384), BF16)
            qkdd = SBx(ph2, "qkdd", (128, 1024), BF16)

            with ExitStack() as pqk:
                p_mA = PSx(pqk, "p_mA", (128, 1024))
                p_mB = PSx(pqk, "p_mB", (128, 1024))

                def mk_silu_evac(dst):
                    def evac(oc, half, pap, mm_T):
                        def _(s):
                            em.wait(s, "s", "T", mm_T)
                            em.inc(s.activation(
                                out=dst[:, oc * 1024 + half * 512: oc * 1024 + (half + 1) * 512],
                                in_=pap, func=AF.Silu), "S")
                        block.scalar(_)
                        return ("S", em.cnt["S"])
                    return evac

                q_allT = fm_mains(wfm[:, 0:2048], 16, mk_silu_evac(Qbuf), XLbuf, XL_READY,
                                  p_mA, p_mB, [("V", xl_list[0])], 0, rhs_list=xl_list)
                Q_S = em.cnt["S"]
                k_allT = fm_mains(wfm[:, 2048:4096], 16, mk_silu_evac(Kbuf), XLbuf, XL_READY,
                                  p_mA, p_mB, [("S", Q_S)], q_allT)
                K_S = em.cnt["S"]

            if "q" in taps:
                def _(sy):
                    em.wait(sy, "sy", "S", Q_S)
                    (dx.pre(sy, "sy") or dx.issue(sy.dma_start(out=DT["tap_q"], in_=Qbuf[:, :])))
                block.sync(_)
            if "k" in taps:
                def _(sy):
                    em.wait(sy, "sy", "S", K_S)
                    (dx.pre(sy, "sy") or dx.issue(sy.dma_start(out=DT["tap_k"], in_=Kbuf[:, :])))
                block.sync(_)

            with ExitStack() as phv:
                p_vA = PSx(phv, "p_vA", (128, 1024))
                p_vB = PSx(phv, "p_vB", (128, 1024))
                p_vt = PSx(phv, "p_vt", (128, 512))
                vfm = SBx(phv, "vfm", (128, 512), BF16)
                vt_state = {"vt_free": 0, "vfm_free": [0, 0]}

                def ev_v(oc, half, pap, mm_T):
                    par = 0
                    def _(v, oc=oc, half=half, par=par, pap=pap, mm_T=mm_T):
                        em.wait(v, "v", "T", mm_T)
                        em.wait(v, "v", "T", vt_state["vfm_free"][par])
                        em.inc(v.tensor_copy(out=vfm[:, par * 512:(par + 1) * 512], in_=pap), "V")
                    block.vector(_)
                    cp_V = em.cnt["V"]

                    def _(t, oc=oc, half=half, par=par, cp_V=cp_V):
                        em.wait(t, "t", "V", cp_V)
                        em.wait(t, "t", "V", vt_state["vt_free"])
                        wait_consts(t, "t")
                        for tt in range(4):
                            i = t.matmul(p_vt[:, tt * 128:(tt + 1) * 128],
                                         vfm[:, par * 512 + tt * 128: par * 512 + (tt + 1) * 128],
                                         ident_bf[:, :], start=True, stop=True)
                        em.inc(i, "T")
                    block.tensor(_)
                    vt_state["vfm_free"][par] = em.cnt["T"]
                    tr_T = em.cnt["T"]

                    def _(v, oc=oc, half=half, tr_T=tr_T):
                        em.wait(v, "v", "T", tr_T)
                        for tt in range(4):
                            tc = half * 4 + tt
                            i = v.tensor_copy(out=VTbuf[:, tc * 2048 + oc * 128: tc * 2048 + (oc + 1) * 128],
                                              in_=p_vt[:, tt * 128:(tt + 1) * 128])
                        em.inc(i, "V")
                    block.vector(_)
                    vt_state["vt_free"] = em.cnt["V"]
                    return ("V", cp_V)

                v_allT = fm_mains(wv, 16, ev_v, XLbuf, XL_READY,
                                  p_vA, p_vB, [("S", K_S)], k_allT)
            VT_V = em.cnt["V"]
            if "vt" in taps:
                def _(sy):
                    em.wait(sy, "sy", "V", VT_V)
                    (dx.pre(sy, "sy") or dx.issue(sy.dma_start(out=DT["tap_vt"], in_=VTbuf[:, :])))
                block.sync(_)

            phau = ExitStack()
            p_u = phau.enter_context(nc.psum_tensor("p_u", [128, 1024], F32))
            u_state = {"slab_dma": [], "slab_cons": {}, "free": [("V", VT_V), ("V", VT_V)], "k": 0, "done_V": 0}

            def emit_u():
                k = u_state["k"]
                if k >= 32:
                    return
                u_state["k"] = k + 1
                oc, half = k // 2, k % 2
                sl = oc // 2
                if k % 4 == 0:
                    def _(g, sl=sl):
                        if sl >= 3:
                            em.wait(g, "g", "T", u_state["slab_cons"][sl - 3])
                        else:
                            em.wait(g, "g", "T", v_allT)
                        src = wfm[:, 4096 + sl * 256: 4096 + (sl + 1) * 256]
                        gw.pre(g, "g")
                        i = g.dma_start(
                            out=WS[sl % 3][:, 0:4096].rearrange("p (c o) -> p c o", c=16),
                            in_=src.rearrange("(c p) o -> p c o", p=128))
                        u_state["slab_dma"].append(gw.issue(i))
                    block.gpsimd(_)

                par = k % 2
                def _(t, k=k, oc=oc, half=half, sl=sl, par=par):
                    nm, v = u_state["slab_dma"][sl]
                    em.wait(t, "t", nm, v)
                    em.wait(t, "t", "V", XL_READY)
                    sem, val = u_state["free"][par]
                    em.wait(t, "t", sem, val)
                    for c in range(16):
                        i = t.matmul(p_u[:, par * 512:(par + 1) * 512],
                                     WS[sl % 3][:, c * 256 + (oc % 2) * 128: c * 256 + (oc % 2) * 128 + 128],
                                     XLbuf[:, c * 1024 + half * 512: c * 1024 + (half + 1) * 512],
                                     start=(c == 0), stop=(c == 15))
                    em.inc(i, "T")
                block.tensor(_)
                u_state["slab_cons"][sl] = em.cnt["T"]
                mm_T = em.cnt["T"]

                def _(v, oc=oc, half=half, par=par, mm_T=mm_T):
                    em.wait(v, "v", "T", mm_T)
                    em.inc(v.tensor_copy(
                        out=Ubuf[:, oc * 1024 + half * 512: oc * 1024 + (half + 1) * 512],
                        in_=p_u[:, par * 512:(par + 1) * 512]), "V")
                block.vector(_)
                u_state["free"][par] = ("V", em.cnt["V"])
                u_state["done_V"] = em.cnt["V"]

            with ExitStack() as phc:
                pkt = [PSx(phc, f"p_kt{i}", (128, 128)) for i in range(4)]
                pkv = [PSx(phc, f"p_kv{i}", (128, 128)) for i in range(2)]
                ktg = SBx(phc, "ktg", (128, 512), BF16)

                last_evac_S = [("V", 0), ("V", 0)]
                last_kv_T = [0, 0]
                for hp in range(8):
                    heads = (2 * hp, 2 * hp + 1)
                    def _(sc, heads=heads, k0=last_kv_T[0], k1=last_kv_T[1]):
                        em.wait(sc, "s", "V", BC_V)
                        wait_consts(sc, "s")
                        em.wait(sc, "s", "T", max(k0, k1))
                        for hi, h in enumerate(heads):
                            i = sc.activation(out=kdg[:, hi * 8:(hi + 1) * 8], in_=kdecg_sb[:, :],
                                              func=AF.Exp, scale=sbcast[:, h:h + 1])
                        em.inc(i, "S")
                    block.scalar(_)
                    tbg_S = em.cnt["S"]

                    ev_pend = {}
                    for c8 in range(9):
                        if c8 < 8:
                            for hi, h in enumerate(heads):
                                ksl = (c8 % 2) * 2 + hi
                                def _(t, hi=hi, h=h, c8=c8, ksl=ksl, ev=last_evac_S[hi]):
                                    em.wait(t, "t", "S", K_S)
                                    em.wait(t, "t", "V", VT_V)
                                    em.wait(t, "t", ev[0], ev[1])
                                    wait_consts(t, "t")
                                    em.inc(t.matmul(pkt[ksl][:, :],
                                                    Kbuf[:, h * 1024 + c8 * 128: h * 1024 + (c8 + 1) * 128],
                                                    ident_bf[:, :], start=True, stop=True), "T")
                                block.tensor(_)
                                kt_T = em.cnt["T"]

                                def _(v, hi=hi, c8=c8, ksl=ksl, kt_T=kt_T, tbg_S=tbg_S, kvp=last_kv_T[hi]):
                                    em.wait(v, "v", "T", kt_T)
                                    em.wait(v, "v", "S", tbg_S)
                                    em.wait(v, "v", "T", kvp)
                                    em.inc(v.tensor_scalar(out=ktg[:, ksl * 128:(ksl + 1) * 128], in0=pkt[ksl][:, :],
                                                           scalar1=kdg[:, hi * 8 + c8: hi * 8 + c8 + 1],
                                                           scalar2=None, op0=ALU.mult), "V")
                                block.vector(_)
                                ev_pend[(c8, hi)] = ("V", em.cnt["V"])
                        if c8 >= 1:
                            cc = c8 - 1
                            for hi, h in enumerate(heads):
                                ksl = (cc % 2) * 2 + hi
                                ev = ev_pend.pop((cc, hi))
                                def _(t, hi=hi, h=h, cc=cc, ksl=ksl, ev=ev):
                                    em.wait(t, "t", ev[0], ev[1])
                                    em.wait(t, "t", "V", VT_V)
                                    em.inc(t.matmul(pkv[hi][:, :], ktg[:, ksl * 128:(ksl + 1) * 128],
                                                    VTbuf[:, cc * 2048 + h * 128: cc * 2048 + (h + 1) * 128],
                                                    start=(cc == 0), stop=(cc == 7)), "T")
                                block.tensor(_)
                                last_kv_T[hi] = em.cnt["T"]
                                last_evac_S[hi] = ev

                    def _(v, heads=heads, t0=last_kv_T[0], t1=last_kv_T[1]):
                        em.wait(v, "v", "T", max(t0, t1))
                        for hi, h in enumerate(heads):
                            i = v.tensor_copy(out=KVIO[:, h * 128:(h + 1) * 128], in_=pkv[hi][:, :])
                        em.inc(i, "V")
                    block.vector(_)
                chain_T = em.cnt["T"]
            KVIO_V = em.cnt["V"]
            if "kvio" in taps:
                def _(sy):
                    em.wait(sy, "sy", "V", KVIO_V)
                    (dx.pre(sy, "sy") or dx.issue(sy.dma_start(out=DT["tap_kvio"], in_=KVIO[:, :])))
                block.sync(_)

            def _(g):
                em.wait(g, "g", "V", KVIO_V)
                (gw.pre(g, "g") or gw.issue(g.dma_start(out=cc_in.ap(), in_=KVIO[:, :])))
            block.gpsimd(_)
            ccin_nm, ccin_v = gw.last()

            def _(g):
                em.wait(g, "g", ccin_nm, ccin_v)
                em.inc(g.collective_compute(
                    "AllGather", ALU.bypass,
                    replica_groups=[[0, 1], [2, 3], [4, 5], [6, 7]],
                    ins=[cc_in.ap()],
                    outs=[cc_out.ap()]), "C")
            block.gpsimd(_)

            def _(sy):
                em.wait(sy, "sy", "C", em.cnt["C"])
                (dx.pre(sy, "sy") or dx.issue(sy.dma_start(out=KVIO[:, :], in_=cc_out[0:128, :])))
            block.sync(_)
            sin_nm, sin_v = dx.last()

            def _(v):
                em.wait(v, "v", sin_nm, sin_v)
                wait_consts(v, "v")
                em.inc(v.tensor_scalar(out=KVIO[:, :], in0=KVIO[:, :], scalar1=mask_sb[:, :],
                                       scalar2=None, op0=ALU.mult), "V")
            block.vector(_)
            SIN_V = em.cnt["V"]
            if "sin" in taps:
                def _(sy):
                    em.wait(sy, "sy", "V", SIN_V)
                    (dx.pre(sy, "sy") or dx.issue(sy.dma_start(out=DT["tap_sin"], in_=KVIO[:, :])))
                block.sync(_)

            while u_state["k"] < 32:
                emit_u()
            phau.close()
            u_allT = em.cnt["T"]
            U_S = ("V", u_state["done_V"])

            with ExitStack() as phr1:
                PQ = [PSx(phr1, f"p_rq{i}", (128, 512)) for i in range(2)]
                PD = [PSx(phr1, f"p_d{i}", (128, 512)) for i in range(2)]
                tblfree = [0, 0]
                qkfree = [0, 0]
                dgfree = [0, 0]
                for h in range(H):
                    ts_ = h % 2
                    def _(sc, h=h, ts_=ts_, tf=tblfree[ts_]):
                        em.wait(sc, "s", "V", BC_V)
                        wait_consts(sc, "s")
                        em.wait(sc, "s", "V", tf)
                        em.inc(sc.activation(out=tbl[:, ts_ * 512:(ts_ + 1) * 512], in_=gneg_sb[:, :],
                                             func=AF.Exp, scale=sbcast[:, h:h + 1]), "S")
                    block.scalar(_)
                    tb_S = em.cnt["S"]

                    for j in range(NBLK):
                        sl = (h * 4 + j) % 2
                        qoff = h * 1024 + j * 256
                        def _(t, h=h, j=j, sl=sl, qoff=qoff, qf=qkfree[sl]):
                            em.wait(t, "t", "S", Q_S)
                            em.wait(t, "t", "V", qf)
                            for sh in range(2):
                                i = t.matmul(PQ[sl][:, sh * 256:(sh + 1) * 256],
                                             Kbuf[:, qoff + sh * 128: qoff + sh * 128 + 128],
                                             Qbuf[:, qoff:qoff + 256], start=True, stop=True)
                            em.inc(i, "T")
                        block.tensor(_)
                        qk_T = em.cnt["T"]

                        def _(v, sl=sl, ts_=ts_, qk_T=qk_T, tb_S=tb_S):
                            em.wait(v, "v", "T", qk_T)
                            em.wait(v, "v", "S", tb_S)
                            em.inc(v.tensor_mul(out=qkdd[:, sl * 512:(sl + 1) * 512],
                                                in0=PQ[sl][:, :], in1=tbl[:, ts_ * 512:(ts_ + 1) * 512]), "V")
                        block.vector(_)
                        qkfree[sl] = em.cnt["V"]
                        tblfree[ts_] = em.cnt["V"]
                        qdd_V = em.cnt["V"]

                        def _(t, h=h, j=j, sl=sl, qv=qdd_V, df=dgfree[sl]):
                            em.wait(t, "t", "V", qv)
                            em.wait(t, "t", "V", df)
                            for sh in range(2):
                                i = t.matmul(PD[sl][:, 0:256],
                                             VTbuf[:, (2 * j + sh) * 2048 + h * 128: (2 * j + sh) * 2048 + (h + 1) * 128],
                                             qkdd[:, sl * 512 + sh * 256: sl * 512 + (sh + 1) * 256],
                                             start=(sh == 0), stop=(sh == 1))
                            em.inc(i, "T")
                        block.tensor(_)
                        dg_T = em.cnt["T"]

                        def _(v, h=h, j=j, sl=sl, dg_T=dg_T):
                            em.wait(v, "v", "T", dg_T)
                            em.inc(v.tensor_copy(out=XLbuf[:, h * 1024 + j * 256: h * 1024 + (j + 1) * 256],
                                                 in_=PD[sl][:, 0:256]), "V")
                        block.vector(_)
                        dgfree[sl] = em.cnt["V"]
            R1_V = em.cnt["V"]

            with ExitStack() as phr2:
                PN = [PSx(phr2, f"p_n{i}", (128, 512)) for i in range(3)]
                PK = [PSx(phr2, f"p_k{i}", (128, 512)) for i in range(3)]
                qrow = SBx(phr2, "qrow", (1, 768), BF16)
                tmpb = SBx(phr2, "tmpb", (128, 256), BF16)
                ndev_V = [R1_V] * 3
                ktl_V = [("V", 0)] * 3
                state_V = [0, 0, 0]
                kvbf_V = [("V", 0)] * 3
                qd_T = [0, 0, 0]
                tmp_V = 0
                for g in range((H + 2) // 3):
                    hs = list(range(3 * g, min(3 * g + 3, H)))
                    tbS_of = {}
                    for h in hs:
                        sl = h % 3
                        def _(sc, h=h, sl=sl, qt=qd_T[sl], kf=ktl_V[sl]):
                            em.wait(sc, "s", "V", BC_V)
                            wait_consts(sc, "s")
                            em.wait(sc, "s", "T", qt)
                            em.wait(sc, "s", kf[0], kf[1])
                            sc.activation(out=qrow[0:1, sl * 256:(sl + 1) * 256], in_=gneg_sb[0:1, 0:256],
                                          func=AF.Exp, scale=sbcast[0:1, h:h + 1])
                            em.inc(sc.activation(out=kdl[:, sl * 2:sl * 2 + 2], in_=kdecl_sb[:, :],
                                                 func=AF.Exp, scale=sbcast[:, h:h + 1]), "S")
                        block.scalar(_)
                        tb_S = em.cnt["S"]

                        def _(t, sl=sl, tb_S=tb_S, nf=ndev_V[sl]):
                            em.wait(t, "t", "S", tb_S)
                            em.wait(t, "t", "V", nf)
                            em.inc(t.matmul(PN[sl][:, 0:256], ones_bfr[0:1, :],
                                            qrow[0:1, sl * 256:(sl + 1) * 256],
                                            start=True, stop=True), "T")
                        block.tensor(_)
                        qd_T[sl] = em.cnt["T"]

                        def _(v, h=h, sl=sl, qdv=qd_T[sl]):
                            em.wait(v, "v", "T", qdv)
                            v.tensor_scalar(out=tbl[:, sl * 256:(sl + 1) * 256], in0=PN[sl][:, 0:256],
                                            scalar1=e1b[:, h:h + 1], scalar2=None, op0=ALU.mult)
                            em.wait(v, "v", "V", SIN_V)
                            v.tensor_copy(out=kvst[:, sl * 128:(sl + 1) * 128],
                                          in_=KVIO[:, h * 128:(h + 1) * 128])
                            em.inc(v.tensor_copy(out=kvbf[:, sl * 128:(sl + 1) * 128],
                                                 in_=kvst[:, sl * 128:(sl + 1) * 128]), "V")
                        block.vector(_)
                        state_V[sl] = em.cnt["V"]
                        kvbf_V[sl] = ("V", em.cnt["V"])
                        tbS_of[h] = tb_S

                    for j in range(NBLK):
                        kv_T_of = {}
                        for h in hs:
                            sl = h % 3
                            qoff = h * 1024 + j * 256
                            def _(t, h=h, j=j, sl=sl, qoff=qoff, kbv=kvbf_V[sl], nf=ndev_V[sl]):
                                em.wait(t, "t", kbv[0], kbv[1])
                                em.wait(t, "t", "V", nf)
                                em.inc(t.matmul(PN[sl][:, 0:256], kvbf[:, sl * 128:(sl + 1) * 128],
                                                Qbuf[:, qoff:qoff + 256], start=True, stop=True), "T")
                            block.tensor(_)
                            nd_T = em.cnt["T"]

                            if j < NBLK - 1:
                                def _(t, h=h, j=j, sl=sl, qoff=qoff, kev=ktl_V[sl], sv=state_V[sl]):
                                    em.wait(t, "t", kev[0], kev[1])
                                    em.wait(t, "t", "V", sv)
                                    for sh in range(2):
                                        i = t.matmul(PK[sl][:, sh * 128:(sh + 1) * 128],
                                                     Kbuf[:, qoff + sh * 128: qoff + sh * 128 + 128],
                                                     ident_bf[:, :], start=True, stop=True)
                                    em.inc(i, "T")
                                block.tensor(_)
                            kt_T = em.cnt["T"]

                            def _(v, h=h, j=j, sl=sl, nd_T=nd_T, tv=tmp_V):
                                em.wait(v, "v", "T", nd_T)
                                v.tensor_mul(out=tmpb[:, :], in0=PN[sl][:, 0:256],
                                             in1=tbl[:, sl * 256:(sl + 1) * 256])
                                em.inc(v.tensor_add(out=XLbuf[:, h * 1024 + j * 256: h * 1024 + (j + 1) * 256],
                                                    in0=XLbuf[:, h * 1024 + j * 256: h * 1024 + (j + 1) * 256],
                                                    in1=tmpb[:, :]), "V")
                            block.vector(_)
                            ndev_V[sl] = em.cnt["V"]
                            tmp_V = em.cnt["V"]

                            if j < NBLK - 1:
                                def _(v, h=h, sl=sl, kt_T=kt_T, tb_S=tbS_of[h]):
                                    em.wait(v, "v", "T", kt_T)
                                    em.wait(v, "v", "S", tb_S)
                                    for sh in range(2):
                                        i = v.tensor_scalar(out=qkdd[:, sl * 256 + sh * 128: sl * 256 + (sh + 1) * 128],
                                                            in0=PK[sl][:, sh * 128:(sh + 1) * 128],
                                                            scalar1=kdl[:, sl * 2 + sh: sl * 2 + sh + 1],
                                                            scalar2=None, op0=ALU.mult)
                                    em.inc(i, "V")
                                block.vector(_)
                                ktl_V[sl] = ("V", em.cnt["V"])

                        if j < NBLK - 1:
                            for h in hs:
                                sl = h % 3
                                def _(t, h=h, j=j, sl=sl, ev=ktl_V[sl]):
                                    em.wait(t, "t", ev[0], ev[1])
                                    for sh in range(2):
                                        i = t.matmul(PK[sl][:, 256:384],
                                                     qkdd[:, sl * 256 + sh * 128: sl * 256 + (sh + 1) * 128],
                                                     VTbuf[:, (2 * j + sh) * 2048 + h * 128: (2 * j + sh) * 2048 + (h + 1) * 128],
                                                     start=(sh == 0), stop=(sh == 1))
                                    em.inc(i, "T")
                                block.tensor(_)
                                kv_T = em.cnt["T"]

                                def _(v, h=h, sl=sl, kv_T=kv_T):
                                    em.wait(v, "v", "T", kv_T)
                                    v.scalar_tensor_tensor(
                                        out=kvst[:, sl * 128:(sl + 1) * 128],
                                        in0=kvst[:, sl * 128:(sl + 1) * 128],
                                        scalar=bdbcast[:, h:h + 1],
                                        in1=PK[sl][:, 256:384],
                                        op0=ALU.mult, op1=ALU.add)
                                    em.inc(v.tensor_copy(out=kvbf[:, sl * 128:(sl + 1) * 128],
                                                         in_=kvst[:, sl * 128:(sl + 1) * 128]), "V")
                                block.vector(_)
                                state_V[sl] = em.cnt["V"]
                                kvbf_V[sl] = ("V", em.cnt["V"])
            O_V = em.cnt["V"]
            rec_T = em.cnt["T"]
            if "o" in taps:
                def _(sy):
                    em.wait(sy, "sy", "V", O_V)
                    (dx.pre(sy, "sy") or dx.issue(sy.dma_start(out=DT["tap_o"], in_=XLbuf[:, :])))
                block.sync(_)

        for engsec, enm in [(block.tensor, "t"), (block.vector, "v"),
                            (block.scalar, "s"), (block.sync, "sy")]:
            def _(e, enm=enm):
                em.barrier(e, enm)
            engsec(_)

        with ExitStack() as ph3:
            AOUT = SBx(ph3, "aout", (128, 16384))
            XR = SBx(ph3, "xr", (128, 2048))
            R2B = SBx(ph3, "r2b", (128, 1024))
            MR2B = SBx(ph3, "mr2b", (128, 1024))
            RSTD = SBx(ph3, "rstd", (128, 1024))
            osq = SBx(ph3, "osq", (128, 2048), BF16)
            tgo = SBx(ph3, "tgo", (128, 1024), BF16)
            rowm3 = SBx(ph3, "rowm3", (1, 1024))
            rowr3 = SBx(ph3, "rowr3", (1, 1024))
            tn3 = SBx(ph3, "tn3", (128, 1024))

            with ExitStack() as prms:
                p_rms = PSx(prms, "p_rms", (1, 1024))
                p_rstd = PSx(prms, "p_rstd", (128, 1024))
                rms_T = []
                for c in range(NIF):
                    par = c % 2
                    def _(v, c=c, par=par):
                        em.wait(v, "v", "V", O_V)
                        if c >= 2:
                            em.wait(v, "v", "T", rms_T[c - 2])
                        em.inc(v.tensor_mul(out=osq[:, par * 1024:(par + 1) * 1024],
                                            in0=XLbuf[:, c * 1024:(c + 1) * 1024],
                                            in1=XLbuf[:, c * 1024:(c + 1) * 1024]), "V")
                    block.vector(_)
                    sv = em.cnt["V"]

                    def _(t, c=c, par=par, sv=sv):
                        em.wait(t, "t", "V", sv)
                        for half in range(2):
                            i = t.matmul(p_rms[0:1, half * 512:(half + 1) * 512], ones_bf[:, :],
                                         osq[:, par * 1024 + half * 512: par * 1024 + (half + 1) * 512],
                                         start=(c == 0), stop=(c == NIF - 1))
                        em.inc(i, "T")
                    block.tensor(_)
                    rms_T.append(em.cnt["T"])

                def _(v):
                    em.wait(v, "v", "T", rms_T[-1])
                    em.inc(v.tensor_scalar(out=rowr3[0:1, :], in0=p_rms[0:1, :], scalar1=1.0 / 2048,
                                           scalar2=1e-6, op0=ALU.mult, op1=ALU.add), "V")
                block.vector(_)
                vv = em.cnt["V"]

                def _(s):
                    em.wait(s, "s", "V", vv)
                    em.inc(s.sqrt(out=rowr3[0:1, :], in_=rowr3[0:1, :]), "S")
                block.scalar(_)
                sv = em.cnt["S"]

                def _(v):
                    em.wait(v, "v", "S", sv)
                    em.inc(v.reciprocal(out=rowr3[0:1, :], in_=rowr3[0:1, :]), "V")
                block.vector(_)
                vv2 = em.cnt["V"]

                def _(t):
                    em.wait(t, "t", "V", vv2)
                    for half in range(2):
                        i = t.matmul(p_rstd[:, half * 512:(half + 1) * 512], ones_f[0:1, :],
                                     rowr3[0:1, half * 512:(half + 1) * 512], start=True, stop=True)
                    em.inc(i, "T")
                block.tensor(_)
                tb = em.cnt["T"]

                def _(v):
                    em.wait(v, "v", "T", tb)
                    em.inc(v.tensor_copy(out=RSTD[:, :], in_=p_rstd[:, :]), "V")
                block.vector(_)
                RSTD_V = em.cnt["V"]

                def _(v):
                    em.wait(v, "v", "V", RSTD_V)
                    em.wait(v, "v", U_S[0], U_S[1])
                    for c in range(NIF):
                        v.tensor_mul(out=tgo[:, :], in0=XLbuf[:, c * 1024:(c + 1) * 1024],
                                     in1=Ubuf[:, c * 1024:(c + 1) * 1024])
                        em.inc(v.tensor_mul(out=Ubuf[:, c * 1024:(c + 1) * 1024],
                                            in0=tgo[:, :], in1=RSTD[:, :]), "V")
                block.vector(_)
                GO_V = em.cnt["V"]
            if "go" in taps:
                def _(sy):
                    em.wait(sy, "sy", "V", GO_V)
                    (dx.pre(sy, "sy") or dx.issue(sy.dma_start(out=DT["tap_go"], in_=Ubuf[:, :])))
                block.sync(_)

            with ExitStack() as pwo:
                p_wA = PSx(pwo, "p_wA", (128, 1024))
                p_wB = PSx(pwo, "p_wB", (128, 1024))
                p_l2s = PSx(pwo, "p_l2s", (1, 1024))
                p_l2q = PSx(pwo, "p_l2q", (1, 1024))

                st2_T = {}
                pend2 = []

                def ev_w(oc, half, pap, mm_T):
                    par = oc % 2
                    def _(v):
                        em.wait(v, "v", "T", mm_T)
                        if (par, half) in st2_T:
                            em.wait(v, "v", "T", st2_T[(par, half)])
                        v.tensor_copy(out=AOUT[:, oc * 1024 + half * 512: oc * 1024 + (half + 1) * 512], in_=pap)
                        em.inc(v.tensor_mul(out=osq[:, par * 1024 + half * 512: par * 1024 + (half + 1) * 512],
                                            in0=AOUT[:, oc * 1024 + half * 512: oc * 1024 + (half + 1) * 512],
                                            in1=AOUT[:, oc * 1024 + half * 512: oc * 1024 + (half + 1) * 512]), "V")
                    block.vector(_)
                    ss = ("V", em.cnt["V"])

                    def emit_st(oc=oc, half=half, par=par, ss=ss):
                        def _(t):
                            em.wait(t, "t", ss[0], ss[1])
                            t.matmul(p_l2s[0:1, half * 512:(half + 1) * 512], ones_fc[:, :],
                                     AOUT[:, oc * 1024 + half * 512: oc * 1024 + (half + 1) * 512],
                                     start=(oc == 0), stop=(oc == 15))
                            em.inc(t.matmul(p_l2q[0:1, half * 512:(half + 1) * 512], ones_bf[:, :],
                                            osq[:, par * 1024 + half * 512: par * 1024 + (half + 1) * 512],
                                            start=(oc == 0), stop=(oc == 15)), "T")
                        block.tensor(_)
                        st2_T[(par, half)] = em.cnt["T"]
                    pend2.append(emit_st)
                    if len(pend2) > 2:
                        pend2.pop(0)()
                    return ss

                wo_allT = fm_mains(wout, 16, ev_w, Ubuf, GO_V, p_wA, p_wB, [("V", GO_V)], u_allT)
                while pend2:
                    pend2.pop(0)()
                l2_T = em.cnt["T"]
                rows2_V = ln_finalize(p_l2s, p_l2q, rowm3, rowr3, tn3, 1e-5, l2_T)

            if "aout" in taps:
                def _(sy):
                    em.wait(sy, "sy", "T", l2_T)
                    (dx.pre(sy, "sy") or dx.issue(sy.dma_start(out=DT["tap_aout"], in_=AOUT[:, :])))
                block.sync(_)
                tap_aout_dx = dx.last()
            else:
                tap_aout_dx = None

            with ExitStack() as pb2:
                p_R2 = PSx(pb2, "p_R2", (128, 1024))
                p_MR2 = PSx(pb2, "p_MR2", (128, 1024))
                bc2_T = ln_bcast(p_R2, p_MR2, rowr3, rowm3, rows2_V)

                def _(v):
                    em.wait(v, "v", "T", bc2_T)
                    v.tensor_copy(out=R2B[:, :], in_=p_R2[:, :])
                    em.inc(v.tensor_copy(out=MR2B[:, :], in_=p_MR2[:, :]), "V")
                block.vector(_)
                R2_V = em.cnt["V"]

            pl3span = ExitStack()
            p_l3s = pl3span.enter_context(nc.psum_tensor("p_l3s", [1, 1024], F32))
            p_l3q = pl3span.enter_context(nc.psum_tensor("p_l3q", [1, 1024], F32))
            p_R3 = pl3span.enter_context(nc.psum_tensor("p_R3", [128, 1024], F32))
            p_MR3 = pl3span.enter_context(nc.psum_tensor("p_MR3", [128, 1024], F32))
            xr_dma = []
            def _(sy):
                for c in range(2):
                    xr_dma.append((dx.pre(sy, "sy") or dx.issue(sy.dma_start(
                        out=XR[:, c * 1024:(c + 1) * 1024],
                        in_=x_fm[c * 128:(c + 1) * 128, :]))))
            block.sync(_)

            xm_V = []
            l3_T = []
            xmid_spill_c = []
            for c in range(NIF):
                par = c % 2
                def _(v, c=c, par=par):
                    nm, vv = xr_dma[c]
                    em.wait(v, "v", nm, vv)
                    em.wait(v, "v", "V", R2_V)
                    if tap_aout_dx is not None:
                        em.wait(v, "v", tap_aout_dx[0], tap_aout_dx[1])
                    v.tensor_mul(out=tn3[:, :], in0=AOUT[:, c * 1024:(c + 1) * 1024], in1=R2B[:, :])
                    v.tensor_sub(out=tn3[:, :], in0=tn3[:, :], in1=MR2B[:, :])
                    em.inc(v.tensor_add(out=AOUT[:, c * 1024:(c + 1) * 1024],
                                        in0=tn3[:, :], in1=XR[:, par * 1024:(par + 1) * 1024]), "V")
                block.vector(_)
                xm_V.append(em.cnt["V"])

                if c + 2 < NIF:
                    def _(sy, c=c, par=par, xv=xm_V[-1]):
                        em.wait(sy, "sy", "V", xv)
                        xr_dma.append((dx.pre(sy, "sy") or dx.issue(sy.dma_start(
                            out=XR[:, par * 1024:(par + 1) * 1024],
                            in_=x_fm[(c + 2) * 128:(c + 3) * 128, :]))))
                    block.sync(_)

                def _(v, c=c, par=par, xv=xm_V[-1]):
                    em.wait(v, "v", "V", xv)
                    if c >= 2:
                        em.wait(v, "v", "T", l3_T[c - 2])
                    em.inc(v.tensor_mul(out=osq[:, par * 1024:(par + 1) * 1024],
                                        in0=AOUT[:, c * 1024:(c + 1) * 1024],
                                        in1=AOUT[:, c * 1024:(c + 1) * 1024]), "V")
                block.vector(_)
                sq3 = em.cnt["V"]

                def _(g, c=c, xv=xm_V[-1]):
                    em.wait(g, "g", "V", xv)
                    (gw.pre(g, "g") or gw.issue(g.dma_start(out=xmid_sp[c * 128:(c + 1) * 128, :],
                                          in_=AOUT[:, c * 1024:(c + 1) * 1024])))
                    xmid_spill_c.append(gw.last())
                block.gpsimd(_)

                def _(t, c=c, par=par, sq3=sq3):
                    em.wait(t, "t", "V", sq3)
                    for half in range(2):
                        t.matmul(p_l3s[0:1, half * 512:(half + 1) * 512], ones_fc[:, :],
                                 AOUT[:, c * 1024 + half * 512: c * 1024 + (half + 1) * 512],
                                 start=(c == 0), stop=(c == NIF - 1))
                        i = t.matmul(p_l3q[0:1, half * 512:(half + 1) * 512], ones_bf[:, :],
                                     osq[:, par * 1024 + half * 512: par * 1024 + (half + 1) * 512],
                                     start=(c == 0), stop=(c == NIF - 1))
                    em.inc(i, "T")
                block.tensor(_)
                l3_T.append(em.cnt["T"])
            xmid_spill = [(nm, em.cnt[nm]) for nm in gw.names]
            rows3_V = ln_finalize(p_l3s, p_l3q, rowm3, rowr3, tn3, 1e-5, l3_T[-1])
            bc3_T = ln_bcast(p_R3, p_MR3, rowr3, rowm3, rows3_V)
            if "xmid" in taps:
                def _(sy):
                    for nm, v in xmid_spill:
                        em.wait(sy, "sy", nm, v)
                    (dx.pre(sy, "sy") or dx.issue(sy.dma_start(out=DT["tap_xmid"], in_=xmid_sp)))
                block.sync(_)

        for engsec, enm in [(block.tensor, "t"), (block.vector, "v"),
                            (block.scalar, "s"), (block.sync, "sy")]:
            def _(e, enm=enm):
                em.barrier(e, enm)
            engsec(_)
        ffn0_T = em.cnt["T"]

        with ExitStack() as ph4:
            HIDb = SBx(ph4, "hidb", (128, 16384), BF16)
            HIDc = SBx(ph4, "hidc", (128, 11264), BF16)
            ACGs = SBx(ph4, "acgs", (128, 1152), BF16)
            ACUs = SBx(ph4, "acus", (128, 1152), BF16)
            ACDs = SBx(ph4, "acds", (128, 3096), BF16)
            EWX = SBx(ph4, "ewx", (64, 1024))
            LG = SBx(ph4, "lg", (72, 1024), BF16)
            LU = SBx(ph4, "lu", (72, 1024), BF16)
            LD = SBx(ph4, "ld", (72, 1024), BF16)
            SCT = SBx(ph4, "sct", (128, 64))
            BIT = SBx(ph4, "bit", (128, 64))
            EWt = SBx(ph4, "ewt", (128, 64))
            EWF = SBx(ph4, "ewf", (8, 1024), BF16)
            e1 = SBx(ph4, "e1t", (128, 64))
            e2 = SBx(ph4, "e2t", (128, 64))
            b2t = SBx(ph4, "b2t", (128, 64))
            t0t = SBx(ph4, "t0t", (128, 64))
            m1t = SBx(ph4, "m1t", (128, 8))
            w1t = SBx(ph4, "w1t", (128, 8))
            w2t = SBx(ph4, "w2t", (128, 8))
            sst = SBx(ph4, "sst", (128, 8))
            tmt = SBx(ph4, "tmt", (128, 8))
            XMR = SBx(ph4, "xmr", (128, 2048), BF16)
            wrs = SBx(ph4, "wrs", (128, 128), BF16)
            SIL = SBx(ph4, "sil", (128, 1024))
            xms = SBx(ph4, "xms", (128, 2048), BF16)
            Fc = SBx(ph4, "fcr", (128, 2048))
            row4m = SBx(ph4, "row4m", (1, 1024))
            row4r = SBx(ph4, "row4r", (1, 1024))
            tn4 = SBx(ph4, "tn4", (1, 1024))

            Ybuf = XLbuf

            def hid_sl(c, half):
                if c < 16:
                    return Ubuf[:, c * 1024 + half * 512: c * 1024 + (half + 1) * 512]
                if c < 32:
                    return HIDb[:, (c - 16) * 1024 + half * 512: (c - 16) * 1024 + (half + 1) * 512]
                return HIDc[:, (c - 32) * 1024 + half * 512: (c - 32) * 1024 + (half + 1) * 512]

            sw_dx = []
            def _(g):
                for nm, v in xmid_spill:
                    em.wait(g, "g", nm, v)
                em.wait(g, "g", "T", l3_T[-1])
                sw_dx.append((gw.pre(g, "g") or gw.issue(g.dma_start(out=wrs[:, :].rearrange("p (c r) -> p c r", c=16),
                                                  in_=wr.rearrange("(c p) r -> p c r", p=128)))))
                sw_dx.append((gw.pre(g, "g") or gw.issue(g.dma_start(out=ACGs[:, :].rearrange("p (c r) -> p c r", c=16),
                                                  in_=acg.rearrange("(c p) r -> p c r", p=128)))))
                sw_dx.append((gw.pre(g, "g") or gw.issue(g.dma_start(out=ACUs[:, :].rearrange("p (c r) -> p c r", c=16),
                                                  in_=acu.rearrange("(c p) r -> p c r", p=128)))))
                sw_dx.append((gw.pre(g, "g") or gw.issue(g.dma_start(out=ACDs[:, :].rearrange("p (c r) -> p c r", c=43),
                                                  in_=acd.rearrange("(c p) r -> p c r", p=128)))))
            block.gpsimd(_)

            def wait_sw(eng, engname):
                for nm, v in sw_dx:
                    em.wait(eng, engname, nm, v)

            xmr_dma = []
            def _(sy):
                for c in range(2):
                    nm, v = xmid_spill_c[c]
                    em.wait(sy, "sy", nm, v)
                    xmr_dma.append((dx.pre(sy, "sy") or dx.issue(sy.dma_start(
                        out=XMR[:, c * 1024:(c + 1) * 1024],
                        in_=xmid_sp[c * 128:(c + 1) * 128, :]))))
            block.sync(_)

            y_V = []
            for c in range(NIF):
                par = c % 2
                def _(v, c=c, par=par):
                    nm, vv = xmr_dma[c]
                    em.wait(v, "v", nm, vv)
                    em.wait(v, "v", "T", bc3_T)
                    v.tensor_mul(out=SIL[:, :],
                                 in0=XMR[:, par * 1024:(par + 1) * 1024], in1=p_R3[:, :])
                    em.inc(v.tensor_sub(out=Ybuf[:, c * 1024:(c + 1) * 1024],
                                        in0=SIL[:, :], in1=p_MR3[:, :]), "V")
                block.vector(_)
                y_V.append(em.cnt["V"])
                if c + 2 < NIF:
                    def _(sy, c=c, par=par, yv=y_V[-1]):
                        em.wait(sy, "sy", "V", yv)
                        nm, v = xmid_spill_c[c + 2]
                        em.wait(sy, "sy", nm, v)
                        xmr_dma.append((dx.pre(sy, "sy") or dx.issue(sy.dma_start(
                            out=XMR[:, par * 1024:(par + 1) * 1024],
                            in_=xmid_sp[(c + 2) * 128:(c + 3) * 128, :]))))
                    block.sync(_)
            Y_V = em.cnt["V"]
            pl3span.close()
            if "y" in taps:
                def _(sy):
                    em.wait(sy, "sy", "V", Y_V)
                    (dx.pre(sy, "sy") or dx.issue(sy.dma_start(out=DT["tap_y"], in_=Ybuf[:, :])))
                block.sync(_)

            with ExitStack() as prt:
                p_rt = PSx(prt, "p_rt", (8, 1024))
                p_sct = PSx(prt, "p_sct", (128, 16))
                p_ewt = PSx(prt, "p_ewt", (8, 1024))
                p_ewx = PSx(prt, "p_ewx", (64, 1024))

                def _(t):
                    em.wait(t, "t", "V", Y_V)
                    wait_consts(t, "t")
                    wait_sw(t, "t")
                    for c in range(16):
                        for half in range(2):
                            i = t.matmul(p_rt[0:8, half * 512:(half + 1) * 512],
                                         wrs[:, c * 8:(c + 1) * 8],
                                         Ybuf[:, c * 1024 + half * 512: c * 1024 + (half + 1) * 512],
                                         start=(c == 0), stop=(c == 15))
                    em.inc(i, "T")
                block.tensor(_)
                rt_T = em.cnt["T"]

                def _(s):
                    em.wait(s, "s", "T", rt_T)
                    em.inc(s.activation(out=SIL[0:8, :], in_=p_rt[0:8, :], func=AF.Sigmoid), "S")
                block.scalar(_)
                sc_S = em.cnt["S"]

                def _(v):
                    em.wait(v, "v", "S", sc_S)
                    em.inc(v.tensor_scalar(out=Fc[0:8, 0:1024], in0=SIL[0:8, :],
                                           scalar1=ebias_sb[0:8, :], scalar2=None, op0=ALU.add), "V")
                block.vector(_)
                bi_V = em.cnt["V"]

                sct_V = 0
                for c in range(8):
                    def _(t, c=c, pv=sct_V):
                        em.wait(t, "t", "V", bi_V)
                        em.wait(t, "t", "V", pv)
                        t.matmul(p_sct[:, 0:8], SIL[0:8, c * 128:(c + 1) * 128], ident_f[0:8, 0:8],
                                 start=True, stop=True)
                        em.inc(t.matmul(p_sct[:, 8:16], Fc[0:8, c * 128:(c + 1) * 128], ident_f[0:8, 0:8],
                                        start=True, stop=True), "T")
                    block.tensor(_)
                    tT = em.cnt["T"]

                    def _(v, c=c, tT=tT):
                        em.wait(v, "v", "T", tT)
                        v.tensor_copy(out=SCT[:, c * 8:(c + 1) * 8], in_=p_sct[:, 0:8])
                        em.inc(v.tensor_copy(out=BIT[:, c * 8:(c + 1) * 8], in_=p_sct[:, 8:16]), "V")
                    block.vector(_)
                    sct_V = em.cnt["V"]

                def _(v):
                    em.wait(v, "v", "V", sct_V)

                    def step(inst):
                        em.inc(inst, "V")
                        em.wait(v, "v", "V", em.cnt["V"])

                    def g3(ap):
                        return ap.rearrange("p (c e) -> p c e", e=8)

                    def b3(t8):
                        return bass.AP(t8[:, :].tensor, 0, [[8, 128], [1, 8], [0, 8]])

                    step(v.reduce_max(out=m1t[:, :].rearrange("p (c o) -> p c o", o=1),
                                      in_=g3(BIT[:, :]), axis=AX.X))
                    step(v.tensor_tensor(out=g3(e1[:, :]), in0=g3(BIT[:, :]), in1=b3(m1t),
                                         op=ALU.is_equal))
                    step(v.tensor_mul(out=t0t[:, :], in0=e1[:, :], in1=SCT[:, :]))
                    step(v.reduce_sum(out=w1t[:, :].rearrange("p (c o) -> p c o", o=1),
                                      in_=g3(t0t[:, :]), axis=AX.X))
                    step(v.scalar_tensor_tensor(out=b2t[:, :], in0=e1[:, :], scalar=-1e30,
                                                in1=BIT[:, :], op0=ALU.mult, op1=ALU.add))
                    step(v.reduce_max(out=m1t[:, :].rearrange("p (c o) -> p c o", o=1),
                                      in_=g3(b2t[:, :]), axis=AX.X))
                    step(v.tensor_tensor(out=g3(e2[:, :]), in0=g3(b2t[:, :]), in1=b3(m1t),
                                         op=ALU.is_equal))
                    step(v.tensor_mul(out=t0t[:, :], in0=e2[:, :], in1=SCT[:, :]))
                    step(v.reduce_sum(out=w2t[:, :].rearrange("p (c o) -> p c o", o=1),
                                      in_=g3(t0t[:, :]), axis=AX.X))
                    step(v.tensor_add(out=sst[:, :], in0=w1t[:, :], in1=w2t[:, :]))
                    step(v.tensor_scalar(out=sst[:, :], in0=sst[:, :], scalar1=1e-8, scalar2=None,
                                         op0=ALU.add))
                    step(v.reciprocal(out=tmt[:, :], in_=sst[:, :]))
                    step(v.tensor_mul(out=w1t[:, :], in0=w1t[:, :], in1=tmt[:, :]))
                    step(v.tensor_mul(out=w2t[:, :], in0=w2t[:, :], in1=tmt[:, :]))
                    step(v.tensor_tensor(out=g3(t0t[:, :]), in0=g3(e1[:, :]), in1=b3(w1t),
                                         op=ALU.mult))
                    step(v.tensor_tensor(out=g3(e2[:, :]), in0=g3(e2[:, :]), in1=b3(w2t),
                                         op=ALU.mult))
                    step(v.tensor_add(out=EWt[:, :], in0=t0t[:, :], in1=e2[:, :]))
                block.vector(_)
                ew_V = em.cnt["V"]

                def _(t):
                    em.wait(t, "t", "V", ew_V)
                    for c in range(8):
                        i = t.matmul(p_ewt[0:8, c * 128:(c + 1) * 128], EWt[:, c * 8:(c + 1) * 8],
                                     ident_f[:, :], start=True, stop=True)
                    em.inc(i, "T")
                block.tensor(_)
                ewt_T = em.cnt["T"]

                def _(v):
                    em.wait(v, "v", "T", ewt_T)
                    em.inc(v.tensor_copy(out=EWF[:, :], in_=p_ewt[0:8, :]), "V")
                block.vector(_)
                ewf_V = em.cnt["V"]

                def _(t):
                    em.wait(t, "t", "V", ewf_V)
                    for half in range(2):
                        i = t.matmul(p_ewx[0:64, half * 512:(half + 1) * 512], e8_sb[:, :],
                                     EWF[:, half * 512:(half + 1) * 512], start=True, stop=True)
                    em.inc(i, "T")
                block.tensor(_)
                ewx_T = em.cnt["T"]

                def _(v):
                    em.wait(v, "v", "T", ewx_T)
                    em.inc(v.tensor_copy(out=EWX[:, :], in_=p_ewx[0:64, :]), "V")
                block.vector(_)
                EWX_V = em.cnt["V"]
            if "ewx" in taps:
                def _(sy):
                    em.wait(sy, "sy", "V", EWX_V)
                    (dx.pre(sy, "sy") or dx.issue(sy.dma_start(out=DT["tap_ewx"], in_=EWX[:, :])))
                block.sync(_)
            for tp, buf in [("sct", SCT), ("bit", BIT), ("ewt", EWt), ("ewf", EWF)]:
                if tp in taps:
                    def _(sy, tp=tp, buf=buf):
                        em.wait(sy, "sy", "V", EWX_V)
                        (dx.pre(sy, "sy") or dx.issue(sy.dma_start(out=DT["tap_" + tp], in_=buf[:, :])))
                    block.sync(_)

            with ExitStack() as pla:
                p_lg = PSx(pla, "p_lg", (72, 1024))
                p_lu = PSx(pla, "p_lu", (72, 1024))

                def _(t):
                    wait_sw(t, "t")
                    em.wait(t, "t", "V", Y_V)
                    em.wait(t, "t", "V", EWX_V)
                    for c in range(16):
                        for half in range(2):
                            t.matmul(p_lg[:, half * 512:(half + 1) * 512], ACGs[:, c * 72:(c + 1) * 72],
                                     Ybuf[:, c * 1024 + half * 512: c * 1024 + (half + 1) * 512],
                                     start=(c == 0), stop=(c == 15))
                            i = t.matmul(p_lu[:, half * 512:(half + 1) * 512], ACUs[:, c * 72:(c + 1) * 72],
                                         Ybuf[:, c * 1024 + half * 512: c * 1024 + (half + 1) * 512],
                                         start=(c == 0), stop=(c == 15))
                    em.inc(i, "T")
                block.tensor(_)
                la_T = em.cnt["T"]

                def _(v):
                    em.wait(v, "v", "T", la_T)
                    em.wait(v, "v", "V", EWX_V)
                    for p_l, Lx in [(p_lg, LG), (p_lu, LU)]:
                        for half in range(2):
                            v.scalar_tensor_tensor(out=Lx[0:64, half * 512:(half + 1) * 512],
                                                   in0=p_l[0:64, half * 512:(half + 1) * 512],
                                                   scalar=0.125, in1=EWX[:, half * 512:(half + 1) * 512],
                                                   op0=ALU.mult, op1=ALU.mult)
                        i = v.tensor_scalar(out=Lx[64:72, :], in0=p_l[64:72, :], scalar1=0.125,
                                            scalar2=None, op0=ALU.mult)
                    em.inc(i, "V")
                block.vector(_)
                L_V = em.cnt["V"]
            for tp, buf in [("lg", LG), ("lu", LU)]:
                if tp in taps:
                    def _(sy, tp=tp, buf=buf):
                        em.wait(sy, "sy", "V", L_V)
                        (dx.pre(sy, "sy") or dx.issue(sy.dma_start(out=DT["tap_" + tp], in_=buf[:, :])))
                    block.sync(_)

            with ExitStack() as pgu:
                PG = [PSx(pgu, f"p_g{i}", (128, 1024)) for i in range(2)]
                PU = [PSx(pgu, f"p_u{i}", (128, 1024)) for i in range(2)]
                gu_dma = []
                gu_cons = {}
                gu_free = {}
                for o in range(NHF):
                    def _(g, o=o):
                        if o >= 3:
                            em.wait(g, "g", "T", gu_cons[o - 3])
                        else:
                            em.wait(g, "g", "T", wo_allT)
                        b = WS[o % 3]
                        dd = []
                        gw.pre(g, "g")
                        dd.append(gw.issue(g.dma_start(
                            out=b[:, 0:2048].rearrange("p (c k) -> p c k", c=16),
                            in_=wg[:, o * 128:(o + 1) * 128].rearrange("(c p) k -> p c k", p=128))))
                        gw.pre(g, "g")
                        dd.append(gw.issue(g.dma_start(out=b[0:72, 2048:2176],
                                                       in_=bcg[:, o * 128:(o + 1) * 128])))
                        gw.pre(g, "g")
                        dd.append(gw.issue(g.dma_start(
                            out=b[:, 2176:4224].rearrange("p (c k) -> p c k", c=16),
                            in_=wu2[:, o * 128:(o + 1) * 128].rearrange("(c p) k -> p c k", p=128))))
                        gw.pre(g, "g")
                        dd.append(gw.issue(g.dma_start(out=b[0:72, 4224:4352],
                                                       in_=bcu[:, o * 128:(o + 1) * 128])))
                        gu_dma.append(dd)
                    block.gpsimd(_)

                    par = o % 2
                    def _(t, o=o, par=par):
                        for nm, v in gu_dma[o]:
                            em.wait(t, "t", nm, v)
                        em.wait(t, "t", "V", Y_V)
                        em.wait(t, "t", "V", L_V)
                        if (o - 2) in gu_free:
                            sem, val = gu_free[o - 2]
                            em.wait(t, "t", sem, val)
                        b = WS[o % 3]
                        for half in range(2):
                            for c in range(16):
                                t.matmul(PG[par][:, half * 512:(half + 1) * 512],
                                         b[:, c * 128:(c + 1) * 128],
                                         Ybuf[:, c * 1024 + half * 512: c * 1024 + (half + 1) * 512],
                                         start=(c == 0), stop=False)
                            t.matmul(PG[par][:, half * 512:(half + 1) * 512],
                                     b[0:72, 2048:2176],
                                     LG[0:72, half * 512:(half + 1) * 512], start=False, stop=True)
                            for c in range(16):
                                t.matmul(PU[par][:, half * 512:(half + 1) * 512],
                                         b[:, 2176 + c * 128: 2176 + (c + 1) * 128],
                                         Ybuf[:, c * 1024 + half * 512: c * 1024 + (half + 1) * 512],
                                         start=(c == 0), stop=False)
                            i = t.matmul(PU[par][:, half * 512:(half + 1) * 512],
                                         b[0:72, 4224:4352],
                                         LU[0:72, half * 512:(half + 1) * 512], start=False, stop=True)
                        em.inc(i, "T")
                    block.tensor(_)
                    gu_cons[o] = em.cnt["T"]

                    def _(s, o=o, par=par, mm=gu_cons[o]):
                        em.wait(s, "s", "T", mm)
                        em.inc(s.activation(out=SIL[:, :], in_=PG[par][:, :], func=AF.Silu), "S")
                    block.scalar(_)
                    sil_S = em.cnt["S"]

                    def _(v, o=o, par=par, sil_S=sil_S):
                        em.wait(v, "v", "S", sil_S)
                        for half in range(2):
                            i = v.tensor_mul(out=hid_sl(o, half), in0=SIL[:, half * 512:(half + 1) * 512],
                                             in1=PU[par][:, half * 512:(half + 1) * 512])
                        em.inc(i, "V")
                    block.vector(_)
                    gu_free[o] = ("V", em.cnt["V"])
                gu_allT = em.cnt["T"]
            HID_V = em.cnt["V"]
            for tp, buf in [("hida", Ubuf), ("hidb", HIDb), ("hidc", HIDc)]:
                if tp in taps:
                    def _(sy, tp=tp, buf=buf):
                        em.wait(sy, "sy", "V", HID_V)
                        (dx.pre(sy, "sy") or dx.issue(sy.dma_start(out=DT["tap_" + tp], in_=buf[:, :])))
                    block.sync(_)

            with ExitStack() as pld:
                p_ld = PSx(pld, "p_ld", (72, 1024))

                def _(t):
                    em.wait(t, "t", "V", HID_V)
                    wait_sw(t, "t")
                    for c in range(NHF):
                        for half in range(2):
                            i = t.matmul(p_ld[:, half * 512:(half + 1) * 512], ACDs[:, c * 72:(c + 1) * 72],
                                         hid_sl(c, half), start=(c == 0), stop=(c == NHF - 1))
                    em.inc(i, "T")
                block.tensor(_)
                ld_T = em.cnt["T"]

                def _(v):
                    em.wait(v, "v", "T", ld_T)
                    em.wait(v, "v", "V", EWX_V)
                    for half in range(2):
                        v.scalar_tensor_tensor(out=LD[0:64, half * 512:(half + 1) * 512],
                                               in0=p_ld[0:64, half * 512:(half + 1) * 512],
                                               scalar=0.125, in1=EWX[:, half * 512:(half + 1) * 512],
                                               op0=ALU.mult, op1=ALU.mult)
                    em.inc(v.tensor_scalar(out=LD[64:72, :], in0=p_ld[64:72, :], scalar1=0.125,
                                           scalar2=None, op0=ALU.mult), "V")
                block.vector(_)
                LD_V = em.cnt["V"]
            if "ld" in taps:
                def _(sy):
                    em.wait(sy, "sy", "V", LD_V)
                    (dx.pre(sy, "sy") or dx.issue(sy.dma_start(out=DT["tap_ld"], in_=LD[:, :])))
                block.sync(_)

            with ExitStack() as pdn:
                p_dA = PSx(pdn, "p_dA", (128, 1024))
                p_dB = PSx(pdn, "p_dB", (128, 1024))
                p_l4s = PSx(pdn, "p_l4s", (1, 1024))
                p_l4q = PSx(pdn, "p_l4q", (1, 1024))

                QS = [(0, 11), (11, 22), (22, 33), (33, 43)]
                dn_dma = []
                dn_cons = {}
                dn_free = {}
                pend4 = []
                l4_T = []
                for o in range(16):
                    par = o % 2
                    ps = p_dA if par == 0 else p_dB
                    for qi, (c0, c1) in enumerate(QS):
                        idx = o * 4 + qi
                        def _(g, o=o, qi=qi, c0=c0, c1=c1, idx=idx):
                            if idx >= 3:
                                em.wait(g, "g", "T", dn_cons[idx - 3])
                            else:
                                em.wait(g, "g", "T", gu_allT)
                            b = WS[idx % 3]
                            gw.pre(g, "g")
                            i = g.dma_start(
                                out=b[:, 0:(c1 - c0) * 128].rearrange("p (c k) -> p c k", c=c1 - c0),
                                in_=wd[c0 * 128:c1 * 128, o * 128:(o + 1) * 128]
                                .rearrange("(c p) k -> p c k", p=128))
                            dd = [gw.issue(i)]
                            if qi == 3:
                                gw.pre(g, "g")
                                i2 = g.dma_start(out=b[0:72, 1408:1536],
                                                 in_=bcd[:, o * 128:(o + 1) * 128])
                                dd.append(gw.issue(i2))
                            dn_dma.append(dd)
                        block.gpsimd(_)

                        def _(t, o=o, qi=qi, c0=c0, c1=c1, idx=idx, ps=ps):
                            for nm, v in dn_dma[idx]:
                                em.wait(t, "t", nm, v)
                            em.wait(t, "t", "V", HID_V)
                            em.wait(t, "t", "V", LD_V)
                            if qi == 0 and (o - 2) in dn_free:
                                sem, val = dn_free[o - 2]
                                em.wait(t, "t", sem, val)
                            b = WS[idx % 3]
                            i = None
                            for half in range(2):
                                for ci, c in enumerate(range(c0, c1)):
                                    i = t.matmul(ps[:, half * 512:(half + 1) * 512],
                                                 b[:, ci * 128:(ci + 1) * 128],
                                                 hid_sl(c, half), start=(qi == 0 and ci == 0), stop=False)
                                if qi == 3:
                                    i = t.matmul(ps[:, half * 512:(half + 1) * 512],
                                                 b[0:72, 1408:1536],
                                                 LD[0:72, half * 512:(half + 1) * 512], start=False, stop=True)
                            em.inc(i, "T")
                        block.tensor(_)
                        dn_cons[idx] = em.cnt["T"]
                    dmm_T = em.cnt["T"]

                    def emit_st4(o=o, par=par, sqv4=None):
                        def _(t):
                            em.wait(t, "t", "V", sqv4)
                            for half in range(2):
                                t.matmul(p_l4s[0:1, half * 512:(half + 1) * 512], ones_bf[:, :],
                                         XLbuf[:, o * 1024 + half * 512: o * 1024 + (half + 1) * 512],
                                         start=(o == 0), stop=(o == 15))
                                i = t.matmul(p_l4q[0:1, half * 512:(half + 1) * 512], ones_bf[:, :],
                                             xms[:, par * 1024 + half * 512: par * 1024 + (half + 1) * 512],
                                             start=(o == 0), stop=(o == 15))
                            em.inc(i, "T")
                        block.tensor(_)
                        l4_T.append(em.cnt["T"])

                    if pend4:
                        pend4.pop(0)()

                    def _(v, o=o, par=par, ps=ps, dmm_T=dmm_T):
                        em.wait(v, "v", "T", dmm_T)
                        v.tensor_copy(out=XLbuf[:, o * 1024:(o + 1) * 1024], in_=ps[:, :])
                        em.inc(v.tensor_mul(out=xms[:, par * 1024:(par + 1) * 1024],
                                            in0=XLbuf[:, o * 1024:(o + 1) * 1024],
                                            in1=XLbuf[:, o * 1024:(o + 1) * 1024]), "V")
                    block.vector(_)
                    fev_S = ("V", em.cnt["V"])
                    pend4.append(lambda o=o, par=par, s4=em.cnt["V"]: emit_st4(o, par, s4))
                    dn_free[o] = fev_S
                while pend4:
                    pend4.pop(0)()
                rows4_V = ln_finalize(p_l4s, p_l4q, row4m, row4r, tn4, 1e-5, l4_T[-1])
            if "f" in taps:
                def _(sy):
                    em.wait(sy, "sy", "V", dn_free[15][1])
                    (dx.pre(sy, "sy") or dx.issue(sy.dma_start(
                        out=DT["tap_f"].rearrange("(c p) t -> p c t", p=128),
                        in_=XLbuf[:, :].rearrange("p (c t) -> p c t", c=16))))
                block.sync(_)

            with ExitStack() as pfin:
                p_R4 = PSx(pfin, "p_R4", (128, 1024))
                p_MR4 = PSx(pfin, "p_MR4", (128, 1024))
                bc4_T = ln_bcast(p_R4, p_MR4, row4r, row4m, rows4_V)
                tnf = SIL

                xm2_dma = []
                def _(sy):
                    em.wait(sy, "sy", "V", Y_V)
                    for c in range(2):
                        nm, v = xmid_spill_c[c]
                        em.wait(sy, "sy", nm, v)
                        xm2_dma.append((dx.pre(sy, "sy") or dx.issue(sy.dma_start(
                            out=XMR[:, c * 1024:(c + 1) * 1024],
                            in_=xmid_sp[c * 128:(c + 1) * 128, :]))))
                block.sync(_)

                out_dx = []
                fin_V = []
                for c in range(NIF):
                    par = c % 2
                    def _(v, c=c, par=par):
                        nm, vv = xm2_dma[c]
                        em.wait(v, "v", nm, vv)
                        em.wait(v, "v", "T", bc4_T)
                        em.wait(v, "v", "V", dn_free[c][1])
                        v.tensor_mul(out=tnf[:, :], in0=XLbuf[:, c * 1024:(c + 1) * 1024], in1=p_R4[:, :])
                        v.tensor_sub(out=tnf[:, :], in0=tnf[:, :], in1=p_MR4[:, :])
                        em.inc(v.tensor_add(out=Fc[:, par * 1024:(par + 1) * 1024],
                                            in0=tnf[:, :], in1=XMR[:, par * 1024:(par + 1) * 1024]), "V")
                    block.vector(_)
                    fin_V.append(em.cnt["V"])

                    def _(sy, c=c, par=par, fv=fin_V[-1]):
                        em.wait(sy, "sy", "V", fv)
                        out_dx.append((dx.pre(sy, "sy") or dx.issue(sy.dma_start(out=out_fm[c * 128:(c + 1) * 128, :],
                                                            in_=Fc[:, par * 1024:(par + 1) * 1024]))))
                    block.sync(_)

                    if c + 2 < NIF:
                        def _(sy, c=c, par=par):
                            nm, v = xmid_spill_c[c + 2]
                            em.wait(sy, "sy", nm, v)
                            xm2_dma.append((dx.pre(sy, "sy") or dx.issue(sy.dma_start(
                                out=XMR[:, par * 1024:(par + 1) * 1024],
                                in_=xmid_sp[(c + 2) * 128:(c + 3) * 128, :]))))
                        block.sync(_)

                def _(sy):
                    for nm, v in [(nm, em.cnt[nm]) for nm in dx.names]:
                        em.wait(sy, "sy", nm, v)
                    for nm, v in [(nm, em.cnt[nm]) for nm in gw.names]:
                        em.wait(sy, "sy", nm, v)
                    for nm, v in [(nm, em.cnt[nm]) for nm in ds.names]:
                        if v > 0:
                            em.wait(sy, "sy", nm, v)
                block.sync(_)
    return nc


def _prep_core_inputs(core, full, consts_cache={}):
    b, half = core // 2, core % 2
    d = {}
    d["x_fm"] = np.ascontiguousarray(full["L0_attn_x"][b, half * 1024:(half + 1) * 1024, :].T)
    d.update(_host_consts(half))
    return d


def kernel(**inputs):
    f = {k: np.asarray(v) for k, v in inputs.items()}
    Wq = f["L0_attn_W_qkvu"]
    shared = {
        "wfm": np.ascontiguousarray(np.concatenate([Wq[0:4096], Wq[6144:8192]], axis=0).T),
        "wv": np.ascontiguousarray(Wq[4096:6144].T),
        "wout": np.ascontiguousarray(f["L0_attn_W_out"].T),
        "wg": np.ascontiguousarray(f["L0_ffn_W_gate"]),
        "wu2": np.ascontiguousarray(f["L0_ffn_W_up"]),
        "wd": np.ascontiguousarray(f["L0_ffn_W_down"]),
        "wr": np.ascontiguousarray(f["L0_ffn_W_router"]),
        "acg": np.ascontiguousarray(np.concatenate(
            [f["L0_ffn_A_gate"].transpose(1, 0, 2).reshape(2048, 64), f["L0_ffn_sA_gate"]], axis=1)),
        "acu": np.ascontiguousarray(np.concatenate(
            [f["L0_ffn_A_up"].transpose(1, 0, 2).reshape(2048, 64), f["L0_ffn_sA_up"]], axis=1)),
        "acd": np.ascontiguousarray(np.concatenate(
            [f["L0_ffn_A_down"].transpose(1, 0, 2).reshape(5504, 64), f["L0_ffn_sA_down"]], axis=1)),
        "bcg": np.ascontiguousarray(np.concatenate(
            [f["L0_ffn_B_gate"].reshape(64, 5504), f["L0_ffn_sB_gate"]], axis=0)),
        "bcu": np.ascontiguousarray(np.concatenate(
            [f["L0_ffn_B_up"].reshape(64, 5504), f["L0_ffn_sB_up"]], axis=0)),
        "bcd": np.ascontiguousarray(np.concatenate(
            [f["L0_ffn_B_down"].reshape(64, 2048), f["L0_ffn_sB_down"]], axis=0)),
        "slope": np.ascontiguousarray(f["L0_attn_slope_rate"].reshape(1, 16)),
        "ebias": np.ascontiguousarray(f["L0_ffn_expert_bias"].reshape(8, 1)),
    }
    in_maps = []
    for core in range(NCORES):
        d = _prep_core_inputs(core, f)
        d.update(shared)
        in_maps.append(d)

    nc = build()
    res = run_bass_kernel_spmd(nc, in_maps, core_ids=list(range(NCORES)))
    out = np.empty((4, 2048, 2048), dtype=np.float32)
    for core in range(NCORES):
        b, half = core // 2, core % 2
        out[b, half * 1024:(half + 1) * 1024, :] = res.results[core]["out_fm"].T
    return out


# version 27 (final)
# speedup vs baseline: 1.0331x; 1.0331x over previous
import numpy as np
import ml_dtypes
from contextlib import ExitStack
from collections import defaultdict

import concourse.bass as bass
import concourse.mybir as mybir
from concourse.bass_utils import run_bass_kernel_spmd

F32 = mybir.dt.float32
BF16 = mybir.dt.bfloat16
AF = mybir.ActivationFunctionType
ALU = mybir.AluOpType
AX = mybir.AxisListType
BF16NP = ml_dtypes.bfloat16

T = 1024
NIF = 16
H = 16
NBLK = 4
DFF = 5504
NHF = 43
NCORES = 8
MSK_NEG = -1.0e4
WSW = 4352

TAPS = {
    "xl": ((128, 16384), BF16), "q": ((128, 16384), BF16), "k": ((128, 16384), BF16),
    "vt": ((128, 16384), BF16), "u": ((128, 16384), BF16), "kvio": ((128, 2048), F32),
    "sin": ((128, 2048), F32), "o": ((128, 16384), BF16), "go": ((128, 16384), BF16),
    "aout": ((128, 16384), F32), "xmid": ((2048, 1024), BF16), "y": ((128, 16384), BF16),
    "ewx": ((64, 1024), F32), "lg": ((72, 1024), BF16), "lu": ((72, 1024), BF16),
    "hida": ((128, 16384), BF16), "hidb": ((128, 16384), BF16), "hidc": ((128, 11264), BF16),
    "ld": ((72, 1024), BF16), "f": ((2048, 1024), BF16),
    "sct": ((128, 64), F32), "bit": ((128, 64), F32), "ewt": ((128, 64), F32),
    "ewf": ((8, 1024), BF16),
}


class Em:
    def __init__(self, nc, ctx):
        self.nc = nc
        self.cnt = defaultdict(int)
        self.water = defaultdict(int)
        self.sems = {}
        names = (["T", "V", "S", "C"] + [f"gw{i}" for i in range(8)]
                 + [f"dx{i}" for i in range(8)] + [f"ds{i}" for i in range(8)])
        for n in names:
            self.sems[n] = ctx.enter_context(nc.semaphore(n))

    def wait(self, eng, engname, sem, val):
        if val <= 0:
            return
        if self.water[(engname, sem)] >= val:
            return
        eng.wait_ge(self.sems[sem], val)
        self.water[(engname, sem)] = val

    def inc(self, inst, sem, n=1):
        self.cnt[sem] += n
        inst.then_inc(self.sems[sem], n)
        return self.cnt[sem]

    def barrier(self, eng, engname):
        for s in ["T", "V", "S"] + [f"dx{i}" for i in range(8)]:
            self.wait(eng, engname, s, self.cnt[s])


class Ring:
    def __init__(self, em, prefix, n=8):
        self.em = em
        self.names = [f"{prefix}{i}" for i in range(n)]
        self.k = 0
        self.n = n

    def pre(self, eng, engname):
        nm = self.names[self.k % self.n]
        v = self.em.cnt[nm]
        if v > 0:
            self.em.wait(eng, engname, nm, v)

    def issue(self, inst):
        nm = self.names[self.k % self.n]
        self.k += 1
        v = self.em.inc(inst, nm, 16)
        return (nm, v)

    def last(self):
        nm = self.names[(self.k - 1) % self.n]
        return (nm, self.em.cnt[nm])


def _host_consts(half):
    c = {}
    c["c_ident_bf"] = np.eye(128, dtype=np.float32)
    c["c_ident_f"] = np.eye(128, dtype=np.float32)
    c["c_ones_bf"] = np.ones((128, 1), dtype=np.float32)
    c["c_ones_f"] = np.ones((1, 128), dtype=np.float32)
    e8 = np.zeros((8, 64), dtype=np.float32)
    for e in range(8):
        e8[e, e * 8:(e + 1) * 8] = 1
    c["c_e8"] = e8
    g = np.zeros((128, 2, 256), dtype=np.float32)
    for sh in range(2):
        sglob = sh * 128 + np.arange(128)[:, None]
        m = np.arange(256)[None, :]
        g[:, sh, :] = np.where(m >= sglob, -(m - sglob).astype(np.float32), MSK_NEG)
    c["c_gneg"] = np.ascontiguousarray(g.reshape(128, 512))
    c["c_onesr"] = np.ones((1, 128), dtype=np.float32)
    kd = np.zeros((128, 2), dtype=np.float32)
    for sh in range(2):
        kd[:, sh] = (sh * 128 + np.arange(128) - 255).astype(np.float32)
    c["c_kdecl"] = kd
    kg = np.zeros((128, 8), dtype=np.float32)
    for ch in range(8):
        kg[:, ch] = (ch * 128 + np.arange(128) - 1023).astype(np.float32)
    c["c_kdecg"] = kg
    c["c_mask"] = np.full((128, 1), 1.0 if half == 1 else 0.0, dtype=np.float32)
    return c


def build(taps=()):
    nc = bass.Bass("TRN2", target_bir_lowering=False, detect_race_conditions=False)
    DT = {}

    def din(name, shape, dtype=F32):
        DT[name] = nc.dram_tensor(name, list(shape), dtype, kind="ExternalInput").ap()
        return DT[name]

    x_fm = din("x_fm", (2048, 1024))
    wfm = din("wfm", (2048, 6144))
    wv = din("wv", (2048, 2048))
    wout = din("wout", (2048, 2048))
    wg = din("wg", (2048, 5504))
    wu2 = din("wu2", (2048, 5504))
    wd = din("wd", (5504, 2048))
    wr = din("wr", (2048, 8))
    acg = din("acg", (2048, 72))
    acu = din("acu", (2048, 72))
    acd = din("acd", (5504, 72))
    bcg = din("bcg", (72, 5504))
    bcu = din("bcu", (72, 5504))
    bcd = din("bcd", (72, 2048))
    slope = din("slope", (1, 16))
    ebias = din("ebias", (8, 1))
    c_in = {}
    for nm, shp, dt in [("c_ident_bf", (128, 128), F32), ("c_ident_f", (128, 128), F32),
                        ("c_ones_bf", (128, 1), F32), ("c_ones_f", (1, 128), F32),
                        ("c_e8", (8, 64), F32), ("c_gneg", (128, 512), F32),
                        ("c_onesr", (1, 128), F32), ("c_kdecl", (128, 2), F32),
                        ("c_kdecg", (128, 8), F32), ("c_mask", (128, 1), F32)]:
        c_in[nm] = din(nm, shp, dt)

    out_fm = nc.dram_tensor("out_fm", [2048, 1024], F32, kind="ExternalOutput").ap()
    for tp in taps:
        shp, dt = TAPS[tp]
        DT["tap_" + tp] = nc.dram_tensor("tap_" + tp, list(shp), dt, kind="ExternalOutput").ap()

    xmid_sp = nc.dram_tensor("xmid_sp", [2048, 1024], BF16).ap()
    cc_in = nc.dram_tensor("cc_in", [128, 2048], F32)
    cc_out = nc.dram_tensor("cc_out", [256, 2048], F32)

    with ExitStack() as ctx:
        em = Em(nc, ctx)
        gw = Ring(em, "gw")
        dx = Ring(em, "dx")
        ds = Ring(em, "ds")

        def SBx(c, n, shp, dt=F32):
            return c.enter_context(nc.sbuf_tensor(n, list(shp), dt))

        def PSx(c, n, shp):
            return c.enter_context(nc.psum_tensor(n, list(shp), F32))

        ident_bf = SBx(ctx, "ident_bf", (128, 128), BF16)
        ident_f = SBx(ctx, "ident_f", (128, 128))
        ones_bf = SBx(ctx, "ones_bf", (128, 1), BF16)
        ones_fc = SBx(ctx, "ones_fc", (128, 1))
        ones_f = SBx(ctx, "ones_f", (1, 128))
        e8_sb = SBx(ctx, "e8_sb", (8, 64), BF16)
        gneg_sb = SBx(ctx, "gneg_sb", (128, 512), BF16)
        ones_bfr = SBx(ctx, "ones_bfr", (1, 128), BF16)
        e1b = SBx(ctx, "e1b", (128, 16))
        kdecl_sb = SBx(ctx, "kdecl_sb", (128, 2))
        kdecg_sb = SBx(ctx, "kdecg_sb", (128, 8))
        mask_sb = SBx(ctx, "mask_sb", (128, 1))
        slope_sb = SBx(ctx, "slope_sb", (1, 16))
        ebias_sb = SBx(ctx, "ebias_sb", (8, 1))
        sbcast = SBx(ctx, "sbcast", (128, 16))
        bdbcast = SBx(ctx, "bdbcast", (128, 16))
        WS = [SBx(ctx, f"ws{i}", (128, WSW), BF16) for i in range(3)]
        Ubuf = SBx(ctx, "ubuf", (128, 16384), BF16)
        XLbuf = SBx(ctx, "xlbuf", (128, 16384), BF16)

        block = ctx.enter_context(nc.Block())

        def _(g):
            for nm, dst in [("c_ident_bf", ident_bf), ("c_ident_f", ident_f),
                            ("c_ones_bf", ones_bf), ("c_ones_bf", ones_fc),
                            ("c_ones_f", ones_f),
                            ("c_e8", e8_sb), ("c_gneg", gneg_sb),
                            ("c_onesr", ones_bfr), ("c_kdecl", kdecl_sb),
                            ("c_kdecg", kdecg_sb), ("c_mask", mask_sb)]:
                (gw.pre(g, "g") or gw.issue(g.dma_start(out=dst[:, :], in_=c_in[nm])))
            (gw.pre(g, "g") or gw.issue(g.dma_start(out=slope_sb[:, :], in_=slope)))
            (gw.pre(g, "g") or gw.issue(g.dma_start(out=ebias_sb[:, :], in_=ebias)))
        block.gpsimd(_)
        consts_ready = [(nm, em.cnt[nm]) for nm in gw.names if em.cnt[nm] > 0]

        def wait_consts(eng, engname):
            for nm, v in consts_ready:
                em.wait(eng, engname, nm, v)

        with ExitStack() as ph0:
            p_sb = PSx(ph0, "p_sb", (128, 32))

            def _(t):
                wait_consts(t, "t")
                t.matmul(p_sb[:, 0:16], ones_f[0:1, :], slope_sb[0:1, :], start=True, stop=True)
                em.inc(t.matmul(p_sb[:, 16:32], ones_f[0:1, :], slope_sb[0:1, :], start=True, stop=True), "T")
            block.tensor(_)
            tv = em.cnt["T"]

            def _(v):
                em.wait(v, "v", "T", tv)
                v.tensor_copy(out=sbcast[:, :], in_=p_sb[:, 0:16])
                v.tensor_scalar(out=e1b[:, :], in0=p_sb[:, 0:16], scalar1=-1.0,
                                scalar2=None, op0=ALU.mult)
                em.inc(v.tensor_scalar(out=bdbcast[:, :], in0=p_sb[:, 16:32], scalar1=-256.0,
                                       scalar2=None, op0=ALU.mult), "V")
            block.vector(_)
            vv = em.cnt["V"]

            def _(s):
                em.wait(s, "s", "V", vv)
                s.activation(out=e1b[:, :], in_=e1b[:, :], func=AF.Exp)
                em.inc(s.activation(out=bdbcast[:, :], in_=bdbcast[:, :], func=AF.Exp), "S")
            block.scalar(_)
        BC_V = em.cnt["V"]
        BC_S = em.cnt["S"]

        def ln_finalize(p_s, p_q, rowm, rowr, tn1, eps, stat_T):
            def _(v):
                em.wait(v, "v", "T", stat_T)

                def step(inst):
                    em.inc(inst, "V")
                    em.wait(v, "v", "V", em.cnt["V"])
                step(v.tensor_scalar(out=rowm[0:1, :], in0=p_s[0:1, :], scalar1=1.0 / 2048, scalar2=None, op0=ALU.mult))
                step(v.tensor_scalar(out=rowr[0:1, :], in0=p_q[0:1, :], scalar1=1.0 / 2048, scalar2=None, op0=ALU.mult))
                step(v.tensor_mul(out=tn1[0:1, :], in0=rowm[0:1, :], in1=rowm[0:1, :]))
                step(v.tensor_sub(out=rowr[0:1, :], in0=rowr[0:1, :], in1=tn1[0:1, :]))
                em.inc(v.tensor_scalar(out=rowr[0:1, :], in0=rowr[0:1, :], scalar1=eps, scalar2=None, op0=ALU.add), "V")
            block.vector(_)
            v1 = em.cnt["V"]

            def _(s):
                em.wait(s, "s", "V", v1)
                em.inc(s.sqrt(out=rowr[0:1, :], in_=rowr[0:1, :]), "S")
            block.scalar(_)
            s1 = em.cnt["S"]

            def _(v):
                em.wait(v, "v", "S", s1)
                em.inc(v.reciprocal(out=rowr[0:1, :], in_=rowr[0:1, :]), "V")
                em.wait(v, "v", "V", em.cnt["V"])
                em.inc(v.tensor_mul(out=rowm[0:1, :], in0=rowm[0:1, :], in1=rowr[0:1, :]), "V")
            block.vector(_)
            return em.cnt["V"]

        def ln_bcast(p_R, p_MR, rowr, rowm, rows_V):
            def _(t):
                em.wait(t, "t", "V", rows_V)
                for half in range(2):
                    t.matmul(p_R[:, half * 512:(half + 1) * 512], ones_f[0:1, :],
                             rowr[0:1, half * 512:(half + 1) * 512], start=True, stop=True)
                    i = t.matmul(p_MR[:, half * 512:(half + 1) * 512], ones_f[0:1, :],
                                 rowm[0:1, half * 512:(half + 1) * 512], start=True, stop=True)
                em.inc(i, "T")
            block.tensor(_)
            return em.cnt["T"]

        with ExitStack() as ph1:
            xbuf = SBx(ph1, "xbuf", (128, 16384))
            xsq = SBx(ph1, "xsq", (128, 2048), BF16)
            tn = SBx(ph1, "tn", (128, 1024))
            rowm = SBx(ph1, "rowm", (1, 1024))
            rowr = SBx(ph1, "rowr", (1, 1024))
            p_s = PSx(ph1, "p_s1", (1, 1024))
            p_q = PSx(ph1, "p_q1", (1, 1024))
            p_R = PSx(ph1, "p_R1", (128, 1024))
            p_MR = PSx(ph1, "p_MR1", (128, 1024))

            xdma = [None] * NIF
            def _(sy):
                for c in range(0, NIF, 2):
                    xdma[c] = (dx.pre(sy, "sy") or dx.issue(sy.dma_start(
                        out=xbuf[:, c * 1024:(c + 1) * 1024],
                        in_=x_fm[c * 128:(c + 1) * 128, :])))
            block.sync(_)
            def _(sc):
                for c in range(1, NIF, 2):
                    xdma[c] = (ds.pre(sc, "s") or ds.issue(sc.dma_start(
                        out=xbuf[:, c * 1024:(c + 1) * 1024],
                        in_=x_fm[c * 128:(c + 1) * 128, :])))
            block.scalar(_)

            stat_T = []
            for c in range(NIF):
                par = c % 2
                def _(v, c=c, par=par):
                    nm, vv = xdma[c]
                    em.wait(v, "v", nm, vv)
                    if c >= 2:
                        em.wait(v, "v", "T", stat_T[c - 2])
                    em.inc(v.tensor_mul(out=xsq[:, par * 1024:(par + 1) * 1024],
                                        in0=xbuf[:, c * 1024:(c + 1) * 1024],
                                        in1=xbuf[:, c * 1024:(c + 1) * 1024]), "V")
                block.vector(_)
                sqv = em.cnt["V"]

                def _(t, c=c, par=par, sqv=sqv):
                    em.wait(t, "t", "V", sqv)
                    em.wait(t, "t", "V", BC_V)
                    for half in range(2):
                        t.matmul(p_s[0:1, half * 512:(half + 1) * 512], ones_fc[:, :],
                                 xbuf[:, c * 1024 + half * 512: c * 1024 + (half + 1) * 512],
                                 start=(c == 0), stop=(c == NIF - 1))
                    for half in range(2):
                        i = t.matmul(p_q[0:1, half * 512:(half + 1) * 512], ones_bf[:, :],
                                     xsq[:, par * 1024 + half * 512: par * 1024 + (half + 1) * 512],
                                     start=(c == 0), stop=(c == NIF - 1))
                    em.inc(i, "T")
                block.tensor(_)
                stat_T.append(em.cnt["T"])

            rows_V = ln_finalize(p_s, p_q, rowm, rowr, tn, 1e-5, stat_T[-1])
            bc_T = ln_bcast(p_R, p_MR, rowr, rowm, rows_V)

            xl_list = []
            def _(v):
                em.wait(v, "v", "T", bc_T)
                for c in range(NIF):
                    v.tensor_mul(out=tn[:, :], in0=xbuf[:, c * 1024:(c + 1) * 1024], in1=p_R[:, :])
                    em.inc(v.tensor_sub(out=XLbuf[:, c * 1024:(c + 1) * 1024], in0=tn[:, :], in1=p_MR[:, :]), "V")
                    xl_list.append(em.cnt["V"])
            block.vector(_)
            XL_READY = em.cnt["V"]

            if "xl" in taps:
                def _(sy):
                    em.wait(sy, "sy", "V", XL_READY)
                    (dx.pre(sy, "sy") or dx.issue(sy.dma_start(out=DT["tap_xl"], in_=XLbuf[:, :])))
                block.sync(_)

        def fm_mains(src_dram, n_oc, evac, rhs_buf, rhs_ready_V, psA, psB, ps_free0, ws_free0_T,
                     rhs_list=None):
            nslab = n_oc // 2
            slab_dma = []
            slab_cons = {}
            psum_free = {}
            for sl in range(nslab):
                def _(g, sl=sl):
                    if sl >= 3:
                        em.wait(g, "g", "T", slab_cons[sl - 3])
                    else:
                        em.wait(g, "g", "T", ws_free0_T)
                    src = src_dram[:, sl * 256:(sl + 1) * 256]
                    gw.pre(g, "g")
                    i = g.dma_start(
                        out=WS[sl % 3][:, 0:4096].rearrange("p (c o) -> p c o", c=16),
                        in_=src.rearrange("(c p) o -> p c o", p=128))
                    slab_dma.append(gw.issue(i))
                block.gpsimd(_)

                for oc2 in range(2):
                    oc = 2 * sl + oc2
                    for half in range(2):
                        ps = psA if oc % 2 == 0 else psB
                        def _(t, sl=sl, oc=oc, oc2=oc2, half=half, ps=ps):
                            nm, v = slab_dma[sl]
                            em.wait(t, "t", nm, v)
                            if rhs_list is None:
                                em.wait(t, "t", "V", rhs_ready_V)
                            key = (oc - 2, half)
                            if key in psum_free:
                                for sem, val in psum_free[key]:
                                    em.wait(t, "t", sem, val)
                            else:
                                for sem, val in ps_free0:
                                    em.wait(t, "t", sem, val)
                            for c in range(16):
                                if rhs_list is not None:
                                    em.wait(t, "t", "V", rhs_list[c])
                                i = t.matmul(ps[:, half * 512:(half + 1) * 512],
                                             WS[sl % 3][:, c * 256 + oc2 * 128: c * 256 + oc2 * 128 + 128],
                                             rhs_buf[:, c * 1024 + half * 512: c * 1024 + (half + 1) * 512],
                                             start=(c == 0), stop=(c == 15))
                            em.inc(i, "T")
                        block.tensor(_)
                        mm_T = em.cnt["T"]
                        slab_cons[sl] = mm_T
                        ps = psA if oc % 2 == 0 else psB
                        fr = evac(oc, half, ps[:, half * 512:(half + 1) * 512], mm_T)
                        psum_free[(oc, half)] = [fr] if isinstance(fr, tuple) else fr
            return em.cnt["T"]

        with ExitStack() as ph2:
            Qbuf = SBx(ph2, "qbuf", (128, 16384), BF16)
            Kbuf = SBx(ph2, "kbuf", (128, 16384), BF16)
            VTbuf = SBx(ph2, "vtbuf", (128, 16384), BF16)
            KVIO = SBx(ph2, "kvio", (128, 2048))
            tbl = SBx(ph2, "tbl", (128, 1024), BF16)
            kdl = SBx(ph2, "kdl", (128, 6))
            kdg = SBx(ph2, "kdg", (128, 16))
            kvst = SBx(ph2, "kvst", (128, 384))
            kvbf = SBx(ph2, "kvbf", (128, 384), BF16)
            qkdd = SBx(ph2, "qkdd", (128, 1024), BF16)

            with ExitStack() as pqk:
                p_mA = PSx(pqk, "p_mA", (128, 1024))
                p_mB = PSx(pqk, "p_mB", (128, 1024))

                def mk_silu_evac(dst):
                    def evac(oc, half, pap, mm_T):
                        def _(s):
                            em.wait(s, "s", "T", mm_T)
                            em.inc(s.activation(
                                out=dst[:, oc * 1024 + half * 512: oc * 1024 + (half + 1) * 512],
                                in_=pap, func=AF.Silu), "S")
                        block.scalar(_)
                        return ("S", em.cnt["S"])
                    return evac

                q_allT = fm_mains(wfm[:, 0:2048], 16, mk_silu_evac(Qbuf), XLbuf, XL_READY,
                                  p_mA, p_mB, [("V", xl_list[0])], 0, rhs_list=xl_list)
                Q_S = em.cnt["S"]
                k_allT = fm_mains(wfm[:, 2048:4096], 16, mk_silu_evac(Kbuf), XLbuf, XL_READY,
                                  p_mA, p_mB, [("S", Q_S)], q_allT)
                K_S = em.cnt["S"]

            if "q" in taps:
                def _(sy):
                    em.wait(sy, "sy", "S", Q_S)
                    (dx.pre(sy, "sy") or dx.issue(sy.dma_start(out=DT["tap_q"], in_=Qbuf[:, :])))
                block.sync(_)
            if "k" in taps:
                def _(sy):
                    em.wait(sy, "sy", "S", K_S)
                    (dx.pre(sy, "sy") or dx.issue(sy.dma_start(out=DT["tap_k"], in_=Kbuf[:, :])))
                block.sync(_)

            with ExitStack() as phv:
                p_vA = PSx(phv, "p_vA", (128, 1024))
                p_vB = PSx(phv, "p_vB", (128, 1024))
                p_vt = PSx(phv, "p_vt", (128, 512))
                vfm = SBx(phv, "vfm", (128, 512), BF16)
                vt_state = {"vt_free": 0, "vfm_free": [0, 0]}

                def ev_v(oc, half, pap, mm_T):
                    par = 0
                    def _(v, oc=oc, half=half, par=par, pap=pap, mm_T=mm_T):
                        em.wait(v, "v", "T", mm_T)
                        em.wait(v, "v", "T", vt_state["vfm_free"][par])
                        em.inc(v.tensor_copy(out=vfm[:, par * 512:(par + 1) * 512], in_=pap), "V")
                    block.vector(_)
                    cp_V = em.cnt["V"]

                    def _(t, oc=oc, half=half, par=par, cp_V=cp_V):
                        em.wait(t, "t", "V", cp_V)
                        em.wait(t, "t", "V", vt_state["vt_free"])
                        wait_consts(t, "t")
                        for tt in range(4):
                            i = t.matmul(p_vt[:, tt * 128:(tt + 1) * 128],
                                         vfm[:, par * 512 + tt * 128: par * 512 + (tt + 1) * 128],
                                         ident_bf[:, :], start=True, stop=True)
                        em.inc(i, "T")
                    block.tensor(_)
                    vt_state["vfm_free"][par] = em.cnt["T"]
                    tr_T = em.cnt["T"]

                    def _(v, oc=oc, half=half, tr_T=tr_T):
                        em.wait(v, "v", "T", tr_T)
                        for tt in range(4):
                            tc = half * 4 + tt
                            i = v.tensor_copy(out=VTbuf[:, tc * 2048 + oc * 128: tc * 2048 + (oc + 1) * 128],
                                              in_=p_vt[:, tt * 128:(tt + 1) * 128])
                        em.inc(i, "V")
                    block.vector(_)
                    vt_state["vt_free"] = em.cnt["V"]
                    return ("V", cp_V)

                v_allT = fm_mains(wv, 16, ev_v, XLbuf, XL_READY,
                                  p_vA, p_vB, [("S", K_S)], k_allT)
            VT_V = em.cnt["V"]
            if "vt" in taps:
                def _(sy):
                    em.wait(sy, "sy", "V", VT_V)
                    (dx.pre(sy, "sy") or dx.issue(sy.dma_start(out=DT["tap_vt"], in_=VTbuf[:, :])))
                block.sync(_)

            phau = ExitStack()
            p_u = phau.enter_context(nc.psum_tensor("p_u", [128, 1024], F32))
            u_state = {"slab_dma": [], "slab_cons": {}, "free": [("V", VT_V), ("V", VT_V)], "k": 0, "done_V": 0}

            def emit_u():
                k = u_state["k"]
                if k >= 32:
                    return
                u_state["k"] = k + 1
                oc, half = k // 2, k % 2
                sl = oc // 2
                if k % 4 == 0:
                    def _(g, sl=sl):
                        if sl >= 3:
                            em.wait(g, "g", "T", u_state["slab_cons"][sl - 3])
                        else:
                            em.wait(g, "g", "T", v_allT)
                        src = wfm[:, 4096 + sl * 256: 4096 + (sl + 1) * 256]
                        gw.pre(g, "g")
                        i = g.dma_start(
                            out=WS[sl % 3][:, 0:4096].rearrange("p (c o) -> p c o", c=16),
                            in_=src.rearrange("(c p) o -> p c o", p=128))
                        u_state["slab_dma"].append(gw.issue(i))
                    block.gpsimd(_)

                par = k % 2
                def _(t, k=k, oc=oc, half=half, sl=sl, par=par):
                    nm, v = u_state["slab_dma"][sl]
                    em.wait(t, "t", nm, v)
                    em.wait(t, "t", "V", XL_READY)
                    sem, val = u_state["free"][par]
                    em.wait(t, "t", sem, val)
                    for c in range(16):
                        i = t.matmul(p_u[:, par * 512:(par + 1) * 512],
                                     WS[sl % 3][:, c * 256 + (oc % 2) * 128: c * 256 + (oc % 2) * 128 + 128],
                                     XLbuf[:, c * 1024 + half * 512: c * 1024 + (half + 1) * 512],
                                     start=(c == 0), stop=(c == 15))
                    em.inc(i, "T")
                block.tensor(_)
                u_state["slab_cons"][sl] = em.cnt["T"]
                mm_T = em.cnt["T"]

                def _(v, oc=oc, half=half, par=par, mm_T=mm_T):
                    em.wait(v, "v", "T", mm_T)
                    em.inc(v.tensor_copy(
                        out=Ubuf[:, oc * 1024 + half * 512: oc * 1024 + (half + 1) * 512],
                        in_=p_u[:, par * 512:(par + 1) * 512]), "V")
                block.vector(_)
                u_state["free"][par] = ("V", em.cnt["V"])
                u_state["done_V"] = em.cnt["V"]

            with ExitStack() as phc:
                pkt = [PSx(phc, f"p_kt{i}", (128, 128)) for i in range(4)]
                pkv = [PSx(phc, f"p_kv{i}", (128, 128)) for i in range(2)]
                ktg = SBx(phc, "ktg", (128, 512), BF16)

                last_evac_S = [("V", 0), ("V", 0)]
                last_kv_T = [0, 0]
                for hp in range(8):
                    heads = (2 * hp, 2 * hp + 1)
                    def _(sc, heads=heads, k0=last_kv_T[0], k1=last_kv_T[1]):
                        em.wait(sc, "s", "V", BC_V)
                        wait_consts(sc, "s")
                        em.wait(sc, "s", "T", max(k0, k1))
                        for hi, h in enumerate(heads):
                            i = sc.activation(out=kdg[:, hi * 8:(hi + 1) * 8], in_=kdecg_sb[:, :],
                                              func=AF.Exp, scale=sbcast[:, h:h + 1])
                        em.inc(i, "S")
                    block.scalar(_)
                    tbg_S = em.cnt["S"]

                    ev_pend = {}
                    for c8 in range(9):
                        if c8 < 8:
                            for hi, h in enumerate(heads):
                                ksl = (c8 % 2) * 2 + hi
                                def _(t, hi=hi, h=h, c8=c8, ksl=ksl, ev=last_evac_S[hi]):
                                    em.wait(t, "t", "S", K_S)
                                    em.wait(t, "t", "V", VT_V)
                                    em.wait(t, "t", ev[0], ev[1])
                                    wait_consts(t, "t")
                                    em.inc(t.matmul(pkt[ksl][:, :],
                                                    Kbuf[:, h * 1024 + c8 * 128: h * 1024 + (c8 + 1) * 128],
                                                    ident_bf[:, :], start=True, stop=True), "T")
                                block.tensor(_)
                                kt_T = em.cnt["T"]

                                def _(v, hi=hi, c8=c8, ksl=ksl, kt_T=kt_T, tbg_S=tbg_S, kvp=last_kv_T[hi]):
                                    em.wait(v, "v", "T", kt_T)
                                    em.wait(v, "v", "S", tbg_S)
                                    em.wait(v, "v", "T", kvp)
                                    em.inc(v.tensor_scalar(out=ktg[:, ksl * 128:(ksl + 1) * 128], in0=pkt[ksl][:, :],
                                                           scalar1=kdg[:, hi * 8 + c8: hi * 8 + c8 + 1],
                                                           scalar2=None, op0=ALU.mult), "V")
                                block.vector(_)
                                ev_pend[(c8, hi)] = ("V", em.cnt["V"])
                        if c8 >= 1:
                            cc = c8 - 1
                            for hi, h in enumerate(heads):
                                ksl = (cc % 2) * 2 + hi
                                ev = ev_pend.pop((cc, hi))
                                def _(t, hi=hi, h=h, cc=cc, ksl=ksl, ev=ev):
                                    em.wait(t, "t", ev[0], ev[1])
                                    em.wait(t, "t", "V", VT_V)
                                    em.inc(t.matmul(pkv[hi][:, :], ktg[:, ksl * 128:(ksl + 1) * 128],
                                                    VTbuf[:, cc * 2048 + h * 128: cc * 2048 + (h + 1) * 128],
                                                    start=(cc == 0), stop=(cc == 7)), "T")
                                block.tensor(_)
                                last_kv_T[hi] = em.cnt["T"]
                                last_evac_S[hi] = ev

                    def _(v, heads=heads, t0=last_kv_T[0], t1=last_kv_T[1]):
                        em.wait(v, "v", "T", max(t0, t1))
                        for hi, h in enumerate(heads):
                            i = v.tensor_copy(out=KVIO[:, h * 128:(h + 1) * 128], in_=pkv[hi][:, :])
                        em.inc(i, "V")
                    block.vector(_)
                chain_T = em.cnt["T"]
            KVIO_V = em.cnt["V"]
            if "kvio" in taps:
                def _(sy):
                    em.wait(sy, "sy", "V", KVIO_V)
                    (dx.pre(sy, "sy") or dx.issue(sy.dma_start(out=DT["tap_kvio"], in_=KVIO[:, :])))
                block.sync(_)

            def _(g):
                em.wait(g, "g", "V", KVIO_V)
                (gw.pre(g, "g") or gw.issue(g.dma_start(out=cc_in.ap(), in_=KVIO[:, :])))
            block.gpsimd(_)
            ccin_nm, ccin_v = gw.last()

            def _(g):
                em.wait(g, "g", ccin_nm, ccin_v)
                em.inc(g.collective_compute(
                    "AllGather", ALU.bypass,
                    replica_groups=[[0, 1], [2, 3], [4, 5], [6, 7]],
                    ins=[cc_in.ap()],
                    outs=[cc_out.ap()]), "C")
            block.gpsimd(_)

            def _(sy):
                em.wait(sy, "sy", "C", em.cnt["C"])
                (dx.pre(sy, "sy") or dx.issue(sy.dma_start(out=KVIO[:, :], in_=cc_out[0:128, :])))
            block.sync(_)
            sin_nm, sin_v = dx.last()

            def _(v):
                em.wait(v, "v", sin_nm, sin_v)
                wait_consts(v, "v")
                em.inc(v.tensor_scalar(out=KVIO[:, :], in0=KVIO[:, :], scalar1=mask_sb[:, :],
                                       scalar2=None, op0=ALU.mult), "V")
            block.vector(_)
            SIN_V = em.cnt["V"]
            if "sin" in taps:
                def _(sy):
                    em.wait(sy, "sy", "V", SIN_V)
                    (dx.pre(sy, "sy") or dx.issue(sy.dma_start(out=DT["tap_sin"], in_=KVIO[:, :])))
                block.sync(_)

            while u_state["k"] < 32:
                emit_u()
            phau.close()
            u_allT = em.cnt["T"]
            U_S = ("V", u_state["done_V"])

            with ExitStack() as phr1:
                PQ = [PSx(phr1, f"p_rq{i}", (128, 512)) for i in range(2)]
                PD = [PSx(phr1, f"p_d{i}", (128, 512)) for i in range(2)]
                tblfree = [0, 0]
                qkfree = [0, 0]
                dgfree = [0, 0]
                for h in range(H):
                    ts_ = h % 2
                    def _(sc, h=h, ts_=ts_, tf=tblfree[ts_]):
                        em.wait(sc, "s", "V", BC_V)
                        wait_consts(sc, "s")
                        em.wait(sc, "s", "V", tf)
                        em.inc(sc.activation(out=tbl[:, ts_ * 512:(ts_ + 1) * 512], in_=gneg_sb[:, :],
                                             func=AF.Exp, scale=sbcast[:, h:h + 1]), "S")
                    block.scalar(_)
                    tb_S = em.cnt["S"]

                    for j in range(NBLK):
                        sl = (h * 4 + j) % 2
                        qoff = h * 1024 + j * 256
                        def _(t, h=h, j=j, sl=sl, qoff=qoff, qf=qkfree[sl]):
                            em.wait(t, "t", "S", Q_S)
                            em.wait(t, "t", "V", qf)
                            for sh in range(2):
                                i = t.matmul(PQ[sl][:, sh * 256:(sh + 1) * 256],
                                             Kbuf[:, qoff + sh * 128: qoff + sh * 128 + 128],
                                             Qbuf[:, qoff:qoff + 256], start=True, stop=True)
                            em.inc(i, "T")
                        block.tensor(_)
                        qk_T = em.cnt["T"]

                        def _(v, sl=sl, ts_=ts_, qk_T=qk_T, tb_S=tb_S):
                            em.wait(v, "v", "T", qk_T)
                            em.wait(v, "v", "S", tb_S)
                            em.inc(v.tensor_mul(out=qkdd[:, sl * 512:(sl + 1) * 512],
                                                in0=PQ[sl][:, :], in1=tbl[:, ts_ * 512:(ts_ + 1) * 512]), "V")
                        block.vector(_)
                        qkfree[sl] = em.cnt["V"]
                        tblfree[ts_] = em.cnt["V"]
                        qdd_V = em.cnt["V"]

                        def _(t, h=h, j=j, sl=sl, qv=qdd_V, df=dgfree[sl]):
                            em.wait(t, "t", "V", qv)
                            em.wait(t, "t", "V", df)
                            for sh in range(2):
                                i = t.matmul(PD[sl][:, 0:256],
                                             VTbuf[:, (2 * j + sh) * 2048 + h * 128: (2 * j + sh) * 2048 + (h + 1) * 128],
                                             qkdd[:, sl * 512 + sh * 256: sl * 512 + (sh + 1) * 256],
                                             start=(sh == 0), stop=(sh == 1))
                            em.inc(i, "T")
                        block.tensor(_)
                        dg_T = em.cnt["T"]

                        def _(v, h=h, j=j, sl=sl, dg_T=dg_T):
                            em.wait(v, "v", "T", dg_T)
                            em.inc(v.tensor_copy(out=XLbuf[:, h * 1024 + j * 256: h * 1024 + (j + 1) * 256],
                                                 in_=PD[sl][:, 0:256]), "V")
                        block.vector(_)
                        dgfree[sl] = em.cnt["V"]
            R1_V = em.cnt["V"]

            with ExitStack() as phr2:
                PN = [PSx(phr2, f"p_n{i}", (128, 512)) for i in range(3)]
                PK = [PSx(phr2, f"p_k{i}", (128, 512)) for i in range(3)]
                qrow = SBx(phr2, "qrow", (1, 768), BF16)
                tmpb = SBx(phr2, "tmpb", (128, 256), BF16)
                ndev_V = [R1_V] * 3
                ktl_V = [("V", 0)] * 3
                state_V = [0, 0, 0]
                kvbf_V = [("V", 0)] * 3
                qd_T = [0, 0, 0]
                tmp_V = 0
                for g in range((H + 2) // 3):
                    hs = list(range(3 * g, min(3 * g + 3, H)))
                    tbS_of = {}
                    for h in hs:
                        sl = h % 3
                        def _(sc, h=h, sl=sl, qt=qd_T[sl], kf=ktl_V[sl]):
                            em.wait(sc, "s", "V", BC_V)
                            wait_consts(sc, "s")
                            em.wait(sc, "s", "T", qt)
                            em.wait(sc, "s", kf[0], kf[1])
                            sc.activation(out=qrow[0:1, sl * 256:(sl + 1) * 256], in_=gneg_sb[0:1, 0:256],
                                          func=AF.Exp, scale=sbcast[0:1, h:h + 1])
                            em.inc(sc.activation(out=kdl[:, sl * 2:sl * 2 + 2], in_=kdecl_sb[:, :],
                                                 func=AF.Exp, scale=sbcast[:, h:h + 1]), "S")
                        block.scalar(_)
                        tb_S = em.cnt["S"]

                        def _(t, sl=sl, tb_S=tb_S, nf=ndev_V[sl]):
                            em.wait(t, "t", "S", tb_S)
                            em.wait(t, "t", "V", nf)
                            em.inc(t.matmul(PN[sl][:, 0:256], ones_bfr[0:1, :],
                                            qrow[0:1, sl * 256:(sl + 1) * 256],
                                            start=True, stop=True), "T")
                        block.tensor(_)
                        qd_T[sl] = em.cnt["T"]

                        def _(v, h=h, sl=sl, qdv=qd_T[sl]):
                            em.wait(v, "v", "T", qdv)
                            v.tensor_scalar(out=tbl[:, sl * 256:(sl + 1) * 256], in0=PN[sl][:, 0:256],
                                            scalar1=e1b[:, h:h + 1], scalar2=None, op0=ALU.mult)
                            em.wait(v, "v", "V", SIN_V)
                            v.tensor_copy(out=kvst[:, sl * 128:(sl + 1) * 128],
                                          in_=KVIO[:, h * 128:(h + 1) * 128])
                            em.inc(v.tensor_copy(out=kvbf[:, sl * 128:(sl + 1) * 128],
                                                 in_=kvst[:, sl * 128:(sl + 1) * 128]), "V")
                        block.vector(_)
                        state_V[sl] = em.cnt["V"]
                        kvbf_V[sl] = ("V", em.cnt["V"])
                        tbS_of[h] = tb_S

                    for j in range(NBLK):
                        kv_T_of = {}
                        for h in hs:
                            sl = h % 3
                            qoff = h * 1024 + j * 256
                            def _(t, h=h, j=j, sl=sl, qoff=qoff, kbv=kvbf_V[sl], nf=ndev_V[sl]):
                                em.wait(t, "t", kbv[0], kbv[1])
                                em.wait(t, "t", "V", nf)
                                em.inc(t.matmul(PN[sl][:, 0:256], kvbf[:, sl * 128:(sl + 1) * 128],
                                                Qbuf[:, qoff:qoff + 256], start=True, stop=True), "T")
                            block.tensor(_)
                            nd_T = em.cnt["T"]

                            if j < NBLK - 1:
                                def _(t, h=h, j=j, sl=sl, qoff=qoff, kev=ktl_V[sl], sv=state_V[sl]):
                                    em.wait(t, "t", kev[0], kev[1])
                                    em.wait(t, "t", "V", sv)
                                    for sh in range(2):
                                        i = t.matmul(PK[sl][:, sh * 128:(sh + 1) * 128],
                                                     Kbuf[:, qoff + sh * 128: qoff + sh * 128 + 128],
                                                     ident_bf[:, :], start=True, stop=True)
                                    em.inc(i, "T")
                                block.tensor(_)
                            kt_T = em.cnt["T"]

                            def _(v, h=h, j=j, sl=sl, nd_T=nd_T, tv=tmp_V):
                                em.wait(v, "v", "T", nd_T)
                                v.tensor_mul(out=tmpb[:, :], in0=PN[sl][:, 0:256],
                                             in1=tbl[:, sl * 256:(sl + 1) * 256])
                                em.inc(v.tensor_add(out=XLbuf[:, h * 1024 + j * 256: h * 1024 + (j + 1) * 256],
                                                    in0=XLbuf[:, h * 1024 + j * 256: h * 1024 + (j + 1) * 256],
                                                    in1=tmpb[:, :]), "V")
                            block.vector(_)
                            ndev_V[sl] = em.cnt["V"]
                            tmp_V = em.cnt["V"]

                            if j < NBLK - 1:
                                def _(v, h=h, sl=sl, kt_T=kt_T, tb_S=tbS_of[h]):
                                    em.wait(v, "v", "T", kt_T)
                                    em.wait(v, "v", "S", tb_S)
                                    for sh in range(2):
                                        i = v.tensor_scalar(out=qkdd[:, sl * 256 + sh * 128: sl * 256 + (sh + 1) * 128],
                                                            in0=PK[sl][:, sh * 128:(sh + 1) * 128],
                                                            scalar1=kdl[:, sl * 2 + sh: sl * 2 + sh + 1],
                                                            scalar2=None, op0=ALU.mult)
                                    em.inc(i, "V")
                                block.vector(_)
                                ktl_V[sl] = ("V", em.cnt["V"])

                        if j < NBLK - 1:
                            for h in hs:
                                sl = h % 3
                                def _(t, h=h, j=j, sl=sl, ev=ktl_V[sl]):
                                    em.wait(t, "t", ev[0], ev[1])
                                    for sh in range(2):
                                        i = t.matmul(PK[sl][:, 256:384],
                                                     qkdd[:, sl * 256 + sh * 128: sl * 256 + (sh + 1) * 128],
                                                     VTbuf[:, (2 * j + sh) * 2048 + h * 128: (2 * j + sh) * 2048 + (h + 1) * 128],
                                                     start=(sh == 0), stop=(sh == 1))
                                    em.inc(i, "T")
                                block.tensor(_)
                                kv_T = em.cnt["T"]

                                def _(v, h=h, sl=sl, kv_T=kv_T):
                                    em.wait(v, "v", "T", kv_T)
                                    v.scalar_tensor_tensor(
                                        out=kvst[:, sl * 128:(sl + 1) * 128],
                                        in0=kvst[:, sl * 128:(sl + 1) * 128],
                                        scalar=bdbcast[:, h:h + 1],
                                        in1=PK[sl][:, 256:384],
                                        op0=ALU.mult, op1=ALU.add)
                                    em.inc(v.tensor_copy(out=kvbf[:, sl * 128:(sl + 1) * 128],
                                                         in_=kvst[:, sl * 128:(sl + 1) * 128]), "V")
                                block.vector(_)
                                state_V[sl] = em.cnt["V"]
                                kvbf_V[sl] = ("V", em.cnt["V"])
            O_V = em.cnt["V"]
            rec_T = em.cnt["T"]
            if "o" in taps:
                def _(sy):
                    em.wait(sy, "sy", "V", O_V)
                    (dx.pre(sy, "sy") or dx.issue(sy.dma_start(out=DT["tap_o"], in_=XLbuf[:, :])))
                block.sync(_)

        for engsec, enm in [(block.tensor, "t"), (block.vector, "v"),
                            (block.scalar, "s"), (block.sync, "sy")]:
            def _(e, enm=enm):
                em.barrier(e, enm)
            engsec(_)

        with ExitStack() as ph3:
            AOUT = SBx(ph3, "aout", (128, 16384))
            XR = SBx(ph3, "xr", (128, 2048))
            R2B = SBx(ph3, "r2b", (128, 1024))
            MR2B = SBx(ph3, "mr2b", (128, 1024))
            RSTD = SBx(ph3, "rstd", (128, 1024))
            osq = SBx(ph3, "osq", (128, 2048), BF16)
            tgo = SBx(ph3, "tgo", (128, 1024), BF16)
            rowm3 = SBx(ph3, "rowm3", (1, 1024))
            rowr3 = SBx(ph3, "rowr3", (1, 1024))
            tn3 = SBx(ph3, "tn3", (128, 1024))

            with ExitStack() as prms:
                p_rms = PSx(prms, "p_rms", (1, 1024))
                p_rstd = PSx(prms, "p_rstd", (128, 1024))
                rms_T = []
                for c in range(NIF):
                    par = c % 2
                    def _(v, c=c, par=par):
                        em.wait(v, "v", "V", O_V)
                        if c >= 2:
                            em.wait(v, "v", "T", rms_T[c - 2])
                        em.inc(v.tensor_mul(out=osq[:, par * 1024:(par + 1) * 1024],
                                            in0=XLbuf[:, c * 1024:(c + 1) * 1024],
                                            in1=XLbuf[:, c * 1024:(c + 1) * 1024]), "V")
                    block.vector(_)
                    sv = em.cnt["V"]

                    def _(t, c=c, par=par, sv=sv):
                        em.wait(t, "t", "V", sv)
                        for half in range(2):
                            i = t.matmul(p_rms[0:1, half * 512:(half + 1) * 512], ones_bf[:, :],
                                         osq[:, par * 1024 + half * 512: par * 1024 + (half + 1) * 512],
                                         start=(c == 0), stop=(c == NIF - 1))
                        em.inc(i, "T")
                    block.tensor(_)
                    rms_T.append(em.cnt["T"])

                def _(v):
                    em.wait(v, "v", "T", rms_T[-1])
                    em.inc(v.tensor_scalar(out=rowr3[0:1, :], in0=p_rms[0:1, :], scalar1=1.0 / 2048,
                                           scalar2=1e-6, op0=ALU.mult, op1=ALU.add), "V")
                block.vector(_)
                vv = em.cnt["V"]

                def _(s):
                    em.wait(s, "s", "V", vv)
                    em.inc(s.sqrt(out=rowr3[0:1, :], in_=rowr3[0:1, :]), "S")
                block.scalar(_)
                sv = em.cnt["S"]

                def _(v):
                    em.wait(v, "v", "S", sv)
                    em.inc(v.reciprocal(out=rowr3[0:1, :], in_=rowr3[0:1, :]), "V")
                block.vector(_)
                vv2 = em.cnt["V"]

                def _(t):
                    em.wait(t, "t", "V", vv2)
                    for half in range(2):
                        i = t.matmul(p_rstd[:, half * 512:(half + 1) * 512], ones_f[0:1, :],
                                     rowr3[0:1, half * 512:(half + 1) * 512], start=True, stop=True)
                    em.inc(i, "T")
                block.tensor(_)
                tb = em.cnt["T"]

                def _(v):
                    em.wait(v, "v", "T", tb)
                    em.inc(v.tensor_copy(out=RSTD[:, :], in_=p_rstd[:, :]), "V")
                block.vector(_)
                RSTD_V = em.cnt["V"]

                def _(v):
                    em.wait(v, "v", "V", RSTD_V)
                    em.wait(v, "v", U_S[0], U_S[1])
                    for c in range(NIF):
                        v.tensor_mul(out=tgo[:, :], in0=XLbuf[:, c * 1024:(c + 1) * 1024],
                                     in1=Ubuf[:, c * 1024:(c + 1) * 1024])
                        em.inc(v.tensor_mul(out=Ubuf[:, c * 1024:(c + 1) * 1024],
                                            in0=tgo[:, :], in1=RSTD[:, :]), "V")
                block.vector(_)
                GO_V = em.cnt["V"]
            if "go" in taps:
                def _(sy):
                    em.wait(sy, "sy", "V", GO_V)
                    (dx.pre(sy, "sy") or dx.issue(sy.dma_start(out=DT["tap_go"], in_=Ubuf[:, :])))
                block.sync(_)

            with ExitStack() as pwo:
                p_wA = PSx(pwo, "p_wA", (128, 1024))
                p_wB = PSx(pwo, "p_wB", (128, 1024))
                p_l2s = PSx(pwo, "p_l2s", (1, 1024))
                p_l2q = PSx(pwo, "p_l2q", (1, 1024))

                st2_T = {}
                pend2 = []

                def ev_w(oc, half, pap, mm_T):
                    par = oc % 2
                    def _(v):
                        em.wait(v, "v", "T", mm_T)
                        if (par, half) in st2_T:
                            em.wait(v, "v", "T", st2_T[(par, half)])
                        v.tensor_copy(out=AOUT[:, oc * 1024 + half * 512: oc * 1024 + (half + 1) * 512], in_=pap)
                        em.inc(v.tensor_mul(out=osq[:, par * 1024 + half * 512: par * 1024 + (half + 1) * 512],
                                            in0=AOUT[:, oc * 1024 + half * 512: oc * 1024 + (half + 1) * 512],
                                            in1=AOUT[:, oc * 1024 + half * 512: oc * 1024 + (half + 1) * 512]), "V")
                    block.vector(_)
                    ss = ("V", em.cnt["V"])

                    def emit_st(oc=oc, half=half, par=par, ss=ss):
                        def _(t):
                            em.wait(t, "t", ss[0], ss[1])
                            t.matmul(p_l2s[0:1, half * 512:(half + 1) * 512], ones_fc[:, :],
                                     AOUT[:, oc * 1024 + half * 512: oc * 1024 + (half + 1) * 512],
                                     start=(oc == 0), stop=(oc == 15))
                            em.inc(t.matmul(p_l2q[0:1, half * 512:(half + 1) * 512], ones_bf[:, :],
                                            osq[:, par * 1024 + half * 512: par * 1024 + (half + 1) * 512],
                                            start=(oc == 0), stop=(oc == 15)), "T")
                        block.tensor(_)
                        st2_T[(par, half)] = em.cnt["T"]
                    pend2.append(emit_st)
                    if len(pend2) > 2:
                        pend2.pop(0)()
                    return ss

                wo_allT = fm_mains(wout, 16, ev_w, Ubuf, GO_V, p_wA, p_wB, [("V", GO_V)], u_allT)
                while pend2:
                    pend2.pop(0)()
                l2_T = em.cnt["T"]
                rows2_V = ln_finalize(p_l2s, p_l2q, rowm3, rowr3, tn3, 1e-5, l2_T)

            if "aout" in taps:
                def _(sy):
                    em.wait(sy, "sy", "T", l2_T)
                    (dx.pre(sy, "sy") or dx.issue(sy.dma_start(out=DT["tap_aout"], in_=AOUT[:, :])))
                block.sync(_)
                tap_aout_dx = dx.last()
            else:
                tap_aout_dx = None

            with ExitStack() as pb2:
                p_R2 = PSx(pb2, "p_R2", (128, 1024))
                p_MR2 = PSx(pb2, "p_MR2", (128, 1024))
                bc2_T = ln_bcast(p_R2, p_MR2, rowr3, rowm3, rows2_V)

                def _(v):
                    em.wait(v, "v", "T", bc2_T)
                    v.tensor_copy(out=R2B[:, :], in_=p_R2[:, :])
                    em.inc(v.tensor_copy(out=MR2B[:, :], in_=p_MR2[:, :]), "V")
                block.vector(_)
                R2_V = em.cnt["V"]

            pl3span = ExitStack()
            p_l3s = pl3span.enter_context(nc.psum_tensor("p_l3s", [1, 1024], F32))
            p_l3q = pl3span.enter_context(nc.psum_tensor("p_l3q", [1, 1024], F32))
            p_R3 = pl3span.enter_context(nc.psum_tensor("p_R3", [128, 1024], F32))
            p_MR3 = pl3span.enter_context(nc.psum_tensor("p_MR3", [128, 1024], F32))
            xr_dma = []
            def _(sy):
                for c in range(2):
                    xr_dma.append((dx.pre(sy, "sy") or dx.issue(sy.dma_start(
                        out=XR[:, c * 1024:(c + 1) * 1024],
                        in_=x_fm[c * 128:(c + 1) * 128, :]))))
            block.sync(_)

            xm_V = []
            l3_T = []
            xmid_spill_c = []
            for c in range(NIF):
                par = c % 2
                def _(v, c=c, par=par):
                    nm, vv = xr_dma[c]
                    em.wait(v, "v", nm, vv)
                    em.wait(v, "v", "V", R2_V)
                    if tap_aout_dx is not None:
                        em.wait(v, "v", tap_aout_dx[0], tap_aout_dx[1])
                    v.tensor_mul(out=tn3[:, :], in0=AOUT[:, c * 1024:(c + 1) * 1024], in1=R2B[:, :])
                    v.tensor_sub(out=tn3[:, :], in0=tn3[:, :], in1=MR2B[:, :])
                    em.inc(v.tensor_add(out=AOUT[:, c * 1024:(c + 1) * 1024],
                                        in0=tn3[:, :], in1=XR[:, par * 1024:(par + 1) * 1024]), "V")
                block.vector(_)
                xm_V.append(em.cnt["V"])

                if c + 2 < NIF:
                    def _(sy, c=c, par=par, xv=xm_V[-1]):
                        em.wait(sy, "sy", "V", xv)
                        xr_dma.append((dx.pre(sy, "sy") or dx.issue(sy.dma_start(
                            out=XR[:, par * 1024:(par + 1) * 1024],
                            in_=x_fm[(c + 2) * 128:(c + 3) * 128, :]))))
                    block.sync(_)

                def _(v, c=c, par=par, xv=xm_V[-1]):
                    em.wait(v, "v", "V", xv)
                    if c >= 2:
                        em.wait(v, "v", "T", l3_T[c - 2])
                    em.inc(v.tensor_mul(out=osq[:, par * 1024:(par + 1) * 1024],
                                        in0=AOUT[:, c * 1024:(c + 1) * 1024],
                                        in1=AOUT[:, c * 1024:(c + 1) * 1024]), "V")
                block.vector(_)
                sq3 = em.cnt["V"]

                def _(g, c=c, xv=xm_V[-1]):
                    em.wait(g, "g", "V", xv)
                    (gw.pre(g, "g") or gw.issue(g.dma_start(out=xmid_sp[c * 128:(c + 1) * 128, :],
                                          in_=AOUT[:, c * 1024:(c + 1) * 1024])))
                    xmid_spill_c.append(gw.last())
                block.gpsimd(_)

                def _(t, c=c, par=par, sq3=sq3):
                    em.wait(t, "t", "V", sq3)
                    for half in range(2):
                        t.matmul(p_l3s[0:1, half * 512:(half + 1) * 512], ones_fc[:, :],
                                 AOUT[:, c * 1024 + half * 512: c * 1024 + (half + 1) * 512],
                                 start=(c == 0), stop=(c == NIF - 1))
                        i = t.matmul(p_l3q[0:1, half * 512:(half + 1) * 512], ones_bf[:, :],
                                     osq[:, par * 1024 + half * 512: par * 1024 + (half + 1) * 512],
                                     start=(c == 0), stop=(c == NIF - 1))
                    em.inc(i, "T")
                block.tensor(_)
                l3_T.append(em.cnt["T"])
            xmid_spill = [(nm, em.cnt[nm]) for nm in gw.names]
            rows3_V = ln_finalize(p_l3s, p_l3q, rowm3, rowr3, tn3, 1e-5, l3_T[-1])
            bc3_T = ln_bcast(p_R3, p_MR3, rowr3, rowm3, rows3_V)
            if "xmid" in taps:
                def _(sy):
                    for nm, v in xmid_spill:
                        em.wait(sy, "sy", nm, v)
                    (dx.pre(sy, "sy") or dx.issue(sy.dma_start(out=DT["tap_xmid"], in_=xmid_sp)))
                block.sync(_)

        for engsec, enm in [(block.tensor, "t"), (block.vector, "v"),
                            (block.scalar, "s"), (block.sync, "sy")]:
            def _(e, enm=enm):
                em.barrier(e, enm)
            engsec(_)
        ffn0_T = em.cnt["T"]

        with ExitStack() as ph4:
            HIDb = SBx(ph4, "hidb", (128, 16384), BF16)
            HIDc = SBx(ph4, "hidc", (128, 11264), BF16)
            ACGs = SBx(ph4, "acgs", (128, 1152), BF16)
            ACUs = SBx(ph4, "acus", (128, 1152), BF16)
            ACDs = SBx(ph4, "acds", (128, 3096), BF16)
            EWX = SBx(ph4, "ewx", (64, 1024))
            LG = SBx(ph4, "lg", (72, 1024), BF16)
            LU = SBx(ph4, "lu", (72, 1024), BF16)
            LD = SBx(ph4, "ld", (72, 1024), BF16)
            SCT = SBx(ph4, "sct", (128, 64))
            BIT = SBx(ph4, "bit", (128, 64))
            EWt = SBx(ph4, "ewt", (128, 64))
            EWF = SBx(ph4, "ewf", (8, 1024), BF16)
            e1 = SBx(ph4, "e1t", (128, 64))
            e2 = SBx(ph4, "e2t", (128, 64))
            b2t = SBx(ph4, "b2t", (128, 64))
            t0t = SBx(ph4, "t0t", (128, 64))
            m1t = SBx(ph4, "m1t", (128, 8))
            w1t = SBx(ph4, "w1t", (128, 8))
            w2t = SBx(ph4, "w2t", (128, 8))
            sst = SBx(ph4, "sst", (128, 8))
            tmt = SBx(ph4, "tmt", (128, 8))
            XMR = SBx(ph4, "xmr", (128, 4096), BF16)
            wrs = SBx(ph4, "wrs", (128, 128), BF16)
            SIL = SBx(ph4, "sil", (128, 1024))
            xms = SBx(ph4, "xms", (128, 2048), BF16)
            Fc = SBx(ph4, "fcr", (128, 2048))
            row4m = SBx(ph4, "row4m", (1, 1024))
            row4r = SBx(ph4, "row4r", (1, 1024))
            tn4 = SBx(ph4, "tn4", (1, 1024))

            Ybuf = XLbuf

            def hid_sl(c, half):
                if c < 16:
                    return Ubuf[:, c * 1024 + half * 512: c * 1024 + (half + 1) * 512]
                if c < 32:
                    return HIDb[:, (c - 16) * 1024 + half * 512: (c - 16) * 1024 + (half + 1) * 512]
                return HIDc[:, (c - 32) * 1024 + half * 512: (c - 32) * 1024 + (half + 1) * 512]

            sw_dx = []
            def _(g):
                for nm, v in xmid_spill:
                    em.wait(g, "g", nm, v)
                em.wait(g, "g", "T", l3_T[-1])
                sw_dx.append((gw.pre(g, "g") or gw.issue(g.dma_start(out=wrs[:, :].rearrange("p (c r) -> p c r", c=16),
                                                  in_=wr.rearrange("(c p) r -> p c r", p=128)))))
                sw_dx.append((gw.pre(g, "g") or gw.issue(g.dma_start(out=ACGs[:, :].rearrange("p (c r) -> p c r", c=16),
                                                  in_=acg.rearrange("(c p) r -> p c r", p=128)))))
                sw_dx.append((gw.pre(g, "g") or gw.issue(g.dma_start(out=ACUs[:, :].rearrange("p (c r) -> p c r", c=16),
                                                  in_=acu.rearrange("(c p) r -> p c r", p=128)))))
                sw_dx.append((gw.pre(g, "g") or gw.issue(g.dma_start(out=ACDs[:, :].rearrange("p (c r) -> p c r", c=43),
                                                  in_=acd.rearrange("(c p) r -> p c r", p=128)))))
            block.gpsimd(_)

            def wait_sw(eng, engname):
                for nm, v in sw_dx:
                    em.wait(eng, engname, nm, v)

            xmr_dma = []
            def _(sy):
                for c in range(4):
                    nm, v = xmid_spill_c[c]
                    em.wait(sy, "sy", nm, v)
                    xmr_dma.append((dx.pre(sy, "sy") or dx.issue(sy.dma_start(
                        out=XMR[:, c * 1024:(c + 1) * 1024],
                        in_=xmid_sp[c * 128:(c + 1) * 128, :]))))
            block.sync(_)

            y_V = []
            for c in range(NIF):
                par = c % 4
                def _(v, c=c, par=par):
                    nm, vv = xmr_dma[c]
                    em.wait(v, "v", nm, vv)
                    em.wait(v, "v", "T", bc3_T)
                    v.tensor_mul(out=SIL[:, :],
                                 in0=XMR[:, par * 1024:(par + 1) * 1024], in1=p_R3[:, :])
                    em.inc(v.tensor_sub(out=Ybuf[:, c * 1024:(c + 1) * 1024],
                                        in0=SIL[:, :], in1=p_MR3[:, :]), "V")
                block.vector(_)
                y_V.append(em.cnt["V"])
                if c + 4 < NIF:
                    def _(sy, c=c, par=par, yv=y_V[-1]):
                        em.wait(sy, "sy", "V", yv)
                        nm, v = xmid_spill_c[c + 4]
                        em.wait(sy, "sy", nm, v)
                        xmr_dma.append((dx.pre(sy, "sy") or dx.issue(sy.dma_start(
                            out=XMR[:, par * 1024:(par + 1) * 1024],
                            in_=xmid_sp[(c + 4) * 128:(c + 5) * 128, :]))))
                    block.sync(_)
            Y_V = em.cnt["V"]
            pl3span.close()
            if "y" in taps:
                def _(sy):
                    em.wait(sy, "sy", "V", Y_V)
                    (dx.pre(sy, "sy") or dx.issue(sy.dma_start(out=DT["tap_y"], in_=Ybuf[:, :])))
                block.sync(_)

            with ExitStack() as pla:
                p_lg = PSx(pla, "p_lg", (72, 1024))
                p_lu = PSx(pla, "p_lu", (72, 1024))

                with ExitStack() as prtA:
                    p_rt = PSx(prtA, "p_rt", (8, 1024))
                    p_sct = PSx(prtA, "p_sct", (128, 128))

                    def _(t):
                        em.wait(t, "t", "V", Y_V)
                        wait_consts(t, "t")
                        wait_sw(t, "t")
                        for c in range(16):
                            for half in range(2):
                                i = t.matmul(p_rt[0:8, half * 512:(half + 1) * 512],
                                             wrs[:, c * 8:(c + 1) * 8],
                                             Ybuf[:, c * 1024 + half * 512: c * 1024 + (half + 1) * 512],
                                             start=(c == 0), stop=(c == 15))
                        em.inc(i, "T")
                    block.tensor(_)
                    rt_T = em.cnt["T"]

                    def _(s):
                        em.wait(s, "s", "T", rt_T)
                        em.inc(s.activation(out=SIL[0:8, :], in_=p_rt[0:8, :], func=AF.Sigmoid), "S")
                    block.scalar(_)
                    sc_S = em.cnt["S"]

                    def _(v):
                        em.wait(v, "v", "S", sc_S)
                        em.inc(v.tensor_scalar(out=Fc[0:8, 0:1024], in0=SIL[0:8, :],
                                               scalar1=ebias_sb[0:8, :], scalar2=None, op0=ALU.add), "V")
                    block.vector(_)
                    bi_V = em.cnt["V"]

                    def _(t):
                        em.wait(t, "t", "V", bi_V)
                        for c in range(8):
                            t.matmul(p_sct[:, c * 16: c * 16 + 8], SIL[0:8, c * 128:(c + 1) * 128],
                                     ident_f[0:8, 0:8], start=True, stop=True)
                            i = t.matmul(p_sct[:, c * 16 + 8: c * 16 + 16], Fc[0:8, c * 128:(c + 1) * 128],
                                         ident_f[0:8, 0:8], start=True, stop=True)
                        em.inc(i, "T")
                    block.tensor(_)
                    sct_T = em.cnt["T"]

                    def _(v):
                        em.wait(v, "v", "T", sct_T)
                        v.tensor_copy(out=SCT[:, :].rearrange("p (c w) -> p c w", w=8),
                                      in_=bass.AP(p_sct[:, :].tensor, 0, [[128, 128], [16, 8], [1, 8]]))
                        em.inc(v.tensor_copy(out=BIT[:, :].rearrange("p (c w) -> p c w", w=8),
                                             in_=bass.AP(p_sct[:, :].tensor, 8, [[128, 128], [16, 8], [1, 8]])), "V")
                    block.vector(_)
                    sct_V = em.cnt["V"]

                    def _(t):
                        wait_sw(t, "t")
                        em.wait(t, "t", "V", Y_V)
                        for c in range(16):
                            for half in range(2):
                                t.matmul(p_lg[:, half * 512:(half + 1) * 512], ACGs[:, c * 72:(c + 1) * 72],
                                         Ybuf[:, c * 1024 + half * 512: c * 1024 + (half + 1) * 512],
                                         start=(c == 0), stop=(c == 15))
                                i = t.matmul(p_lu[:, half * 512:(half + 1) * 512], ACUs[:, c * 72:(c + 1) * 72],
                                             Ybuf[:, c * 1024 + half * 512: c * 1024 + (half + 1) * 512],
                                             start=(c == 0), stop=(c == 15))
                        em.inc(i, "T")
                    block.tensor(_)
                    la_T = em.cnt["T"]

                def _(v):
                    em.wait(v, "v", "V", sct_V)

                    def step(inst):
                        em.inc(inst, "V")
                        em.wait(v, "v", "V", em.cnt["V"])

                    def g3(ap):
                        return ap.rearrange("p (c e) -> p c e", e=8)

                    def b3(t8):
                        return bass.AP(t8[:, :].tensor, 0, [[8, 128], [1, 8], [0, 8]])

                    step(v.reduce_max(out=m1t[:, :].rearrange("p (c o) -> p c o", o=1),
                                      in_=g3(BIT[:, :]), axis=AX.X))
                    step(v.tensor_tensor(out=g3(e1[:, :]), in0=g3(BIT[:, :]), in1=b3(m1t),
                                         op=ALU.is_equal))
                    step(v.tensor_mul(out=t0t[:, :], in0=e1[:, :], in1=SCT[:, :]))
                    step(v.reduce_sum(out=w1t[:, :].rearrange("p (c o) -> p c o", o=1),
                                      in_=g3(t0t[:, :]), axis=AX.X))
                    step(v.scalar_tensor_tensor(out=b2t[:, :], in0=e1[:, :], scalar=-1e30,
                                                in1=BIT[:, :], op0=ALU.mult, op1=ALU.add))
                    step(v.reduce_max(out=m1t[:, :].rearrange("p (c o) -> p c o", o=1),
                                      in_=g3(b2t[:, :]), axis=AX.X))
                    step(v.tensor_tensor(out=g3(e2[:, :]), in0=g3(b2t[:, :]), in1=b3(m1t),
                                         op=ALU.is_equal))
                    step(v.tensor_mul(out=t0t[:, :], in0=e2[:, :], in1=SCT[:, :]))
                    step(v.reduce_sum(out=w2t[:, :].rearrange("p (c o) -> p c o", o=1),
                                      in_=g3(t0t[:, :]), axis=AX.X))
                    step(v.tensor_add(out=sst[:, :], in0=w1t[:, :], in1=w2t[:, :]))
                    step(v.tensor_scalar(out=sst[:, :], in0=sst[:, :], scalar1=1e-8, scalar2=None,
                                         op0=ALU.add))
                    step(v.reciprocal(out=tmt[:, :], in_=sst[:, :]))
                    step(v.tensor_mul(out=w1t[:, :], in0=w1t[:, :], in1=tmt[:, :]))
                    step(v.tensor_mul(out=w2t[:, :], in0=w2t[:, :], in1=tmt[:, :]))
                    step(v.tensor_tensor(out=g3(t0t[:, :]), in0=g3(e1[:, :]), in1=b3(w1t),
                                         op=ALU.mult))
                    step(v.tensor_tensor(out=g3(e2[:, :]), in0=g3(e2[:, :]), in1=b3(w2t),
                                         op=ALU.mult))
                    step(v.tensor_add(out=EWt[:, :], in0=t0t[:, :], in1=e2[:, :]))
                block.vector(_)
                ew_V = em.cnt["V"]

                with ExitStack() as prtB:
                    p_ewt = PSx(prtB, "p_ewt", (8, 1024))
                    p_ewx = PSx(prtB, "p_ewx", (64, 1024))

                    def _(t):
                        em.wait(t, "t", "V", ew_V)
                        for c in range(8):
                            i = t.matmul(p_ewt[0:8, c * 128:(c + 1) * 128], EWt[:, c * 8:(c + 1) * 8],
                                         ident_f[:, :], start=True, stop=True)
                        em.inc(i, "T")
                    block.tensor(_)
                    ewt_T = em.cnt["T"]

                    def _(v):
                        em.wait(v, "v", "T", ewt_T)
                        em.inc(v.tensor_copy(out=EWF[:, :], in_=p_ewt[0:8, :]), "V")
                    block.vector(_)
                    ewf_V = em.cnt["V"]

                    def _(t):
                        em.wait(t, "t", "V", ewf_V)
                        for half in range(2):
                            i = t.matmul(p_ewx[0:64, half * 512:(half + 1) * 512], e8_sb[:, :],
                                         EWF[:, half * 512:(half + 1) * 512], start=True, stop=True)
                        em.inc(i, "T")
                    block.tensor(_)
                    ewx_T = em.cnt["T"]

                    def _(v):
                        em.wait(v, "v", "T", ewx_T)
                        em.inc(v.tensor_copy(out=EWX[:, :], in_=p_ewx[0:64, :]), "V")
                    block.vector(_)
                    EWX_V = em.cnt["V"]
                if "ewx" in taps:
                    def _(sy):
                        em.wait(sy, "sy", "V", EWX_V)
                        (dx.pre(sy, "sy") or dx.issue(sy.dma_start(out=DT["tap_ewx"], in_=EWX[:, :])))
                    block.sync(_)
                for tp, buf in [("sct", SCT), ("bit", BIT), ("ewt", EWt), ("ewf", EWF)]:
                    if tp in taps:
                        def _(sy, tp=tp, buf=buf):
                            em.wait(sy, "sy", "V", EWX_V)
                            (dx.pre(sy, "sy") or dx.issue(sy.dma_start(out=DT["tap_" + tp], in_=buf[:, :])))
                        block.sync(_)

                def _(v):
                    em.wait(v, "v", "T", la_T)
                    em.wait(v, "v", "V", EWX_V)
                    for p_l, Lx in [(p_lg, LG), (p_lu, LU)]:
                        for half in range(2):
                            v.scalar_tensor_tensor(out=Lx[0:64, half * 512:(half + 1) * 512],
                                                   in0=p_l[0:64, half * 512:(half + 1) * 512],
                                                   scalar=0.125, in1=EWX[:, half * 512:(half + 1) * 512],
                                                   op0=ALU.mult, op1=ALU.mult)
                        i = v.tensor_scalar(out=Lx[64:72, :], in0=p_l[64:72, :], scalar1=0.125,
                                            scalar2=None, op0=ALU.mult)
                    em.inc(i, "V")
                block.vector(_)
                L_V = em.cnt["V"]
            for tp, buf in [("lg", LG), ("lu", LU)]:
                if tp in taps:
                    def _(sy, tp=tp, buf=buf):
                        em.wait(sy, "sy", "V", L_V)
                        (dx.pre(sy, "sy") or dx.issue(sy.dma_start(out=DT["tap_" + tp], in_=buf[:, :])))
                    block.sync(_)

            with ExitStack() as pgu:
                PG = [PSx(pgu, f"p_g{i}", (128, 1024)) for i in range(2)]
                PU = [PSx(pgu, f"p_u{i}", (128, 1024)) for i in range(2)]
                gu_dma = []
                gu_cons = {}
                gu_free = {}
                for o in range(NHF):
                    def _(g, o=o):
                        if o >= 3:
                            em.wait(g, "g", "T", gu_cons[o - 3])
                        else:
                            em.wait(g, "g", "T", wo_allT)
                        b = WS[o % 3]
                        dd = []
                        gw.pre(g, "g")
                        dd.append(gw.issue(g.dma_start(
                            out=b[:, 0:2048].rearrange("p (c k) -> p c k", c=16),
                            in_=wg[:, o * 128:(o + 1) * 128].rearrange("(c p) k -> p c k", p=128))))
                        gw.pre(g, "g")
                        dd.append(gw.issue(g.dma_start(out=b[0:72, 2048:2176],
                                                       in_=bcg[:, o * 128:(o + 1) * 128])))
                        gw.pre(g, "g")
                        dd.append(gw.issue(g.dma_start(
                            out=b[:, 2176:4224].rearrange("p (c k) -> p c k", c=16),
                            in_=wu2[:, o * 128:(o + 1) * 128].rearrange("(c p) k -> p c k", p=128))))
                        gw.pre(g, "g")
                        dd.append(gw.issue(g.dma_start(out=b[0:72, 4224:4352],
                                                       in_=bcu[:, o * 128:(o + 1) * 128])))
                        gu_dma.append(dd)
                    block.gpsimd(_)

                    par = o % 2
                    def _(t, o=o, par=par):
                        for nm, v in gu_dma[o]:
                            em.wait(t, "t", nm, v)
                        em.wait(t, "t", "V", Y_V)
                        em.wait(t, "t", "V", L_V)
                        if (o - 2) in gu_free:
                            sem, val = gu_free[o - 2]
                            em.wait(t, "t", sem, val)
                        b = WS[o % 3]
                        for half in range(2):
                            for c in range(16):
                                t.matmul(PG[par][:, half * 512:(half + 1) * 512],
                                         b[:, c * 128:(c + 1) * 128],
                                         Ybuf[:, c * 1024 + half * 512: c * 1024 + (half + 1) * 512],
                                         start=(c == 0), stop=False)
                            t.matmul(PG[par][:, half * 512:(half + 1) * 512],
                                     b[0:72, 2048:2176],
                                     LG[0:72, half * 512:(half + 1) * 512], start=False, stop=True)
                            for c in range(16):
                                t.matmul(PU[par][:, half * 512:(half + 1) * 512],
                                         b[:, 2176 + c * 128: 2176 + (c + 1) * 128],
                                         Ybuf[:, c * 1024 + half * 512: c * 1024 + (half + 1) * 512],
                                         start=(c == 0), stop=False)
                            i = t.matmul(PU[par][:, half * 512:(half + 1) * 512],
                                         b[0:72, 4224:4352],
                                         LU[0:72, half * 512:(half + 1) * 512], start=False, stop=True)
                        em.inc(i, "T")
                    block.tensor(_)
                    gu_cons[o] = em.cnt["T"]

                    def _(s, o=o, par=par, mm=gu_cons[o]):
                        em.wait(s, "s", "T", mm)
                        em.inc(s.activation(out=SIL[:, :], in_=PG[par][:, :], func=AF.Silu), "S")
                    block.scalar(_)
                    sil_S = em.cnt["S"]

                    def _(v, o=o, par=par, sil_S=sil_S):
                        em.wait(v, "v", "S", sil_S)
                        for half in range(2):
                            i = v.tensor_mul(out=hid_sl(o, half), in0=SIL[:, half * 512:(half + 1) * 512],
                                             in1=PU[par][:, half * 512:(half + 1) * 512])
                        em.inc(i, "V")
                    block.vector(_)
                    gu_free[o] = ("V", em.cnt["V"])
                gu_allT = em.cnt["T"]
            HID_V = em.cnt["V"]
            for tp, buf in [("hida", Ubuf), ("hidb", HIDb), ("hidc", HIDc)]:
                if tp in taps:
                    def _(sy, tp=tp, buf=buf):
                        em.wait(sy, "sy", "V", HID_V)
                        (dx.pre(sy, "sy") or dx.issue(sy.dma_start(out=DT["tap_" + tp], in_=buf[:, :])))
                    block.sync(_)

            with ExitStack() as pld:
                p_ld = PSx(pld, "p_ld", (72, 1024))

                def _(t):
                    em.wait(t, "t", "V", HID_V)
                    wait_sw(t, "t")
                    for c in range(NHF):
                        for half in range(2):
                            i = t.matmul(p_ld[:, half * 512:(half + 1) * 512], ACDs[:, c * 72:(c + 1) * 72],
                                         hid_sl(c, half), start=(c == 0), stop=(c == NHF - 1))
                    em.inc(i, "T")
                block.tensor(_)
                ld_T = em.cnt["T"]

                def _(v):
                    em.wait(v, "v", "T", ld_T)
                    em.wait(v, "v", "V", EWX_V)
                    for half in range(2):
                        v.scalar_tensor_tensor(out=LD[0:64, half * 512:(half + 1) * 512],
                                               in0=p_ld[0:64, half * 512:(half + 1) * 512],
                                               scalar=0.125, in1=EWX[:, half * 512:(half + 1) * 512],
                                               op0=ALU.mult, op1=ALU.mult)
                    em.inc(v.tensor_scalar(out=LD[64:72, :], in0=p_ld[64:72, :], scalar1=0.125,
                                           scalar2=None, op0=ALU.mult), "V")
                block.vector(_)
                LD_V = em.cnt["V"]
            if "ld" in taps:
                def _(sy):
                    em.wait(sy, "sy", "V", LD_V)
                    (dx.pre(sy, "sy") or dx.issue(sy.dma_start(out=DT["tap_ld"], in_=LD[:, :])))
                block.sync(_)

            with ExitStack() as pdn:
                p_dA = PSx(pdn, "p_dA", (128, 1024))
                p_dB = PSx(pdn, "p_dB", (128, 1024))
                p_l4s = PSx(pdn, "p_l4s", (1, 1024))
                p_l4q = PSx(pdn, "p_l4q", (1, 1024))

                QS = [(0, 11), (11, 22), (22, 33), (33, 43)]
                dn_dma = []
                dn_cons = {}
                dn_free = {}
                pend4 = []
                l4_T = []
                for o in range(16):
                    par = o % 2
                    ps = p_dA if par == 0 else p_dB
                    for qi, (c0, c1) in enumerate(QS):
                        idx = o * 4 + qi
                        def _(g, o=o, qi=qi, c0=c0, c1=c1, idx=idx):
                            if idx >= 3:
                                em.wait(g, "g", "T", dn_cons[idx - 3])
                            else:
                                em.wait(g, "g", "T", gu_allT)
                            b = WS[idx % 3]
                            gw.pre(g, "g")
                            i = g.dma_start(
                                out=b[:, 0:(c1 - c0) * 128].rearrange("p (c k) -> p c k", c=c1 - c0),
                                in_=wd[c0 * 128:c1 * 128, o * 128:(o + 1) * 128]
                                .rearrange("(c p) k -> p c k", p=128))
                            dd = [gw.issue(i)]
                            if qi == 3:
                                gw.pre(g, "g")
                                i2 = g.dma_start(out=b[0:72, 1408:1536],
                                                 in_=bcd[:, o * 128:(o + 1) * 128])
                                dd.append(gw.issue(i2))
                            dn_dma.append(dd)
                        block.gpsimd(_)

                        def _(t, o=o, qi=qi, c0=c0, c1=c1, idx=idx, ps=ps):
                            for nm, v in dn_dma[idx]:
                                em.wait(t, "t", nm, v)
                            em.wait(t, "t", "V", HID_V)
                            em.wait(t, "t", "V", LD_V)
                            if qi == 0 and (o - 2) in dn_free:
                                sem, val = dn_free[o - 2]
                                em.wait(t, "t", sem, val)
                            b = WS[idx % 3]
                            i = None
                            for half in range(2):
                                for ci, c in enumerate(range(c0, c1)):
                                    i = t.matmul(ps[:, half * 512:(half + 1) * 512],
                                                 b[:, ci * 128:(ci + 1) * 128],
                                                 hid_sl(c, half), start=(qi == 0 and ci == 0), stop=False)
                                if qi == 3:
                                    i = t.matmul(ps[:, half * 512:(half + 1) * 512],
                                                 b[0:72, 1408:1536],
                                                 LD[0:72, half * 512:(half + 1) * 512], start=False, stop=True)
                            em.inc(i, "T")
                        block.tensor(_)
                        dn_cons[idx] = em.cnt["T"]
                    dmm_T = em.cnt["T"]

                    def emit_st4(o=o, par=par, sqv4=None):
                        def _(t):
                            em.wait(t, "t", "V", sqv4)
                            for half in range(2):
                                t.matmul(p_l4s[0:1, half * 512:(half + 1) * 512], ones_bf[:, :],
                                         XLbuf[:, o * 1024 + half * 512: o * 1024 + (half + 1) * 512],
                                         start=(o == 0), stop=(o == 15))
                                i = t.matmul(p_l4q[0:1, half * 512:(half + 1) * 512], ones_bf[:, :],
                                             xms[:, par * 1024 + half * 512: par * 1024 + (half + 1) * 512],
                                             start=(o == 0), stop=(o == 15))
                            em.inc(i, "T")
                        block.tensor(_)
                        l4_T.append(em.cnt["T"])

                    if pend4:
                        pend4.pop(0)()

                    def _(v, o=o, par=par, ps=ps, dmm_T=dmm_T):
                        em.wait(v, "v", "T", dmm_T)
                        v.tensor_copy(out=XLbuf[:, o * 1024:(o + 1) * 1024], in_=ps[:, :])
                        em.inc(v.tensor_mul(out=xms[:, par * 1024:(par + 1) * 1024],
                                            in0=XLbuf[:, o * 1024:(o + 1) * 1024],
                                            in1=XLbuf[:, o * 1024:(o + 1) * 1024]), "V")
                    block.vector(_)
                    fev_S = ("V", em.cnt["V"])
                    pend4.append(lambda o=o, par=par, s4=em.cnt["V"]: emit_st4(o, par, s4))
                    dn_free[o] = fev_S
                while pend4:
                    pend4.pop(0)()
                rows4_V = ln_finalize(p_l4s, p_l4q, row4m, row4r, tn4, 1e-5, l4_T[-1])
            if "f" in taps:
                def _(sy):
                    em.wait(sy, "sy", "V", dn_free[15][1])
                    (dx.pre(sy, "sy") or dx.issue(sy.dma_start(
                        out=DT["tap_f"].rearrange("(c p) t -> p c t", p=128),
                        in_=XLbuf[:, :].rearrange("p (c t) -> p c t", c=16))))
                block.sync(_)

            with ExitStack() as pfin:
                p_R4 = PSx(pfin, "p_R4", (128, 1024))
                p_MR4 = PSx(pfin, "p_MR4", (128, 1024))
                bc4_T = ln_bcast(p_R4, p_MR4, row4r, row4m, rows4_V)
                tnf = SIL

                xm2_dma = []
                def _(sy):
                    em.wait(sy, "sy", "V", Y_V)
                    for c in range(4):
                        nm, v = xmid_spill_c[c]
                        em.wait(sy, "sy", nm, v)
                        xm2_dma.append((dx.pre(sy, "sy") or dx.issue(sy.dma_start(
                            out=XMR[:, c * 1024:(c + 1) * 1024],
                            in_=xmid_sp[c * 128:(c + 1) * 128, :]))))
                block.sync(_)

                out_dx = []
                fin_V = []
                for c in range(NIF):
                    par = c % 2
                    p4 = c % 4
                    def _(v, c=c, par=par, p4=p4):
                        nm, vv = xm2_dma[c]
                        em.wait(v, "v", nm, vv)
                        em.wait(v, "v", "T", bc4_T)
                        em.wait(v, "v", "V", dn_free[c][1])
                        v.tensor_mul(out=tnf[:, :], in0=XLbuf[:, c * 1024:(c + 1) * 1024], in1=p_R4[:, :])
                        v.tensor_sub(out=tnf[:, :], in0=tnf[:, :], in1=p_MR4[:, :])
                        em.inc(v.tensor_add(out=Fc[:, par * 1024:(par + 1) * 1024],
                                            in0=tnf[:, :], in1=XMR[:, p4 * 1024:(p4 + 1) * 1024]), "V")
                    block.vector(_)
                    fin_V.append(em.cnt["V"])

                    def _(sy, c=c, par=par, fv=fin_V[-1]):
                        em.wait(sy, "sy", "V", fv)
                        out_dx.append((dx.pre(sy, "sy") or dx.issue(sy.dma_start(out=out_fm[c * 128:(c + 1) * 128, :],
                                                            in_=Fc[:, par * 1024:(par + 1) * 1024]))))
                    block.sync(_)

                    if c + 4 < NIF:
                        def _(sy, c=c, p4=p4):
                            nm, v = xmid_spill_c[c + 4]
                            em.wait(sy, "sy", nm, v)
                            xm2_dma.append((dx.pre(sy, "sy") or dx.issue(sy.dma_start(
                                out=XMR[:, p4 * 1024:(p4 + 1) * 1024],
                                in_=xmid_sp[(c + 4) * 128:(c + 5) * 128, :]))))
                        block.sync(_)

                def _(sy):
                    for nm, v in [(nm, em.cnt[nm]) for nm in dx.names]:
                        em.wait(sy, "sy", nm, v)
                    for nm, v in [(nm, em.cnt[nm]) for nm in gw.names]:
                        em.wait(sy, "sy", nm, v)
                    for nm, v in [(nm, em.cnt[nm]) for nm in ds.names]:
                        if v > 0:
                            em.wait(sy, "sy", nm, v)
                block.sync(_)
    return nc


def _prep_core_inputs(core, full, consts_cache={}):
    b, half = core // 2, core % 2
    d = {}
    d["x_fm"] = np.ascontiguousarray(full["L0_attn_x"][b, half * 1024:(half + 1) * 1024, :].T)
    d.update(_host_consts(half))
    return d


def kernel(**inputs):
    f = {k: np.asarray(v) for k, v in inputs.items()}
    Wq = f["L0_attn_W_qkvu"]
    shared = {
        "wfm": np.ascontiguousarray(np.concatenate([Wq[0:4096], Wq[6144:8192]], axis=0).T),
        "wv": np.ascontiguousarray(Wq[4096:6144].T),
        "wout": np.ascontiguousarray(f["L0_attn_W_out"].T),
        "wg": np.ascontiguousarray(f["L0_ffn_W_gate"]),
        "wu2": np.ascontiguousarray(f["L0_ffn_W_up"]),
        "wd": np.ascontiguousarray(f["L0_ffn_W_down"]),
        "wr": np.ascontiguousarray(f["L0_ffn_W_router"]),
        "acg": np.ascontiguousarray(np.concatenate(
            [f["L0_ffn_A_gate"].transpose(1, 0, 2).reshape(2048, 64), f["L0_ffn_sA_gate"]], axis=1)),
        "acu": np.ascontiguousarray(np.concatenate(
            [f["L0_ffn_A_up"].transpose(1, 0, 2).reshape(2048, 64), f["L0_ffn_sA_up"]], axis=1)),
        "acd": np.ascontiguousarray(np.concatenate(
            [f["L0_ffn_A_down"].transpose(1, 0, 2).reshape(5504, 64), f["L0_ffn_sA_down"]], axis=1)),
        "bcg": np.ascontiguousarray(np.concatenate(
            [f["L0_ffn_B_gate"].reshape(64, 5504), f["L0_ffn_sB_gate"]], axis=0)),
        "bcu": np.ascontiguousarray(np.concatenate(
            [f["L0_ffn_B_up"].reshape(64, 5504), f["L0_ffn_sB_up"]], axis=0)),
        "bcd": np.ascontiguousarray(np.concatenate(
            [f["L0_ffn_B_down"].reshape(64, 2048), f["L0_ffn_sB_down"]], axis=0)),
        "slope": np.ascontiguousarray(f["L0_attn_slope_rate"].reshape(1, 16)),
        "ebias": np.ascontiguousarray(f["L0_ffn_expert_bias"].reshape(8, 1)),
    }
    in_maps = []
    for core in range(NCORES):
        d = _prep_core_inputs(core, f)
        d.update(shared)
        in_maps.append(d)

    nc = build()
    res = run_bass_kernel_spmd(nc, in_maps, core_ids=list(range(NCORES)))
    out = np.empty((4, 2048, 2048), dtype=np.float32)
    for core in range(NCORES):
        b, half = core // 2, core % 2
        out[b, half * 1024:(half + 1) * 1024, :] = res.results[core]["out_fm"].T
    return out
